# version 05
# baseline (speedup 1.0000x reference)
from contextlib import ExitStack

import numpy as np
import ml_dtypes

import concourse.bass as bass
import concourse.mybir as mybir
from concourse.bass_utils import run_bass_kernel_spmd

F32 = mybir.dt.float32
BF16 = mybir.dt.bfloat16
AF = mybir.ActivationFunctionType
ALU = mybir.AluOpType
AX = mybir.AxisListType

_DEBUG = False
_SIM_SAFE = True

NCORES = 8
T = 2048
MS = 1024
NMSLAB = T // MS
SLAB = 256
NSLAB = MS // SLAB
D = 2048
NCH = D // 128
FF = 1024
FC = FF // 128
NEXP = 16
NSH = 2
H = 16
DH = 64
EH = 128
LAM0 = 0.8 - 0.6 * float(np.exp(-0.3 * 12))
LN1M = float(np.log(1.0 - LAM0))
EPS = 1e-5

ENGINES = ("sync", "gpsimd", "vector", "scalar", "tensor")


class Sem:
    def __init__(self, name):
        self.name = name
        self.n = 0
        self.h = None


def build_program():
    nc = bass.Bass("TRN2", target_bir_lowering=False)
    Q = {e: [] for e in ENGINES}
    ld, wv, vd, st, pe, dv, ac, gp = (Sem(s) for s in ("ld", "wv", "vd", "st", "pe", "dv", "ac", "gp"))
    wvb = Sem("wvb"); wv0 = Sem("wv0"); wv1 = Sem("wv1")
    lds = [Sem("lds0"), Sem("lds1")]
    sts = [Sem("sts0"), Sem("sts1")]
    vda = [Sem("vda"), Sem("vdb")]
    allsems = [ld, wv, vd, st, pe, dv, ac, gp, wvb, wv0, wv1] + lds + sts + vda

    def q(eng, fn):
        Q[eng].append(fn)

    eng_sem = {"vector": dv, "scalar": ac}

    def qi(eng, make, sem=None, k=1):
        if _SIM_SAFE and eng in eng_sem:
            own = eng_sem[eng]
            if not (sem is own and k):
                if sem is None or not k:
                    sem, k = own, 1
                else:
                    pass
            v = own.n
            if v > 0:
                q(eng, lambda e, s=own, v=v: e.wait_ge(s.h, v))
        tgt = None
        incs = []
        if sem is not None and k:
            sem.n += k
            tgt = sem.n
            incs.append((sem, k))
        if _SIM_SAFE and eng in eng_sem and eng_sem[eng] is not sem:
            own = eng_sem[eng]
            own.n += 1
            incs.append((own, 1))
        def fn(e, make=make, incs=tuple(incs)):
            inst = make(e)
            for s, kk in incs:
                inst.then_inc(s.h, kk)
        q(eng, fn)
        return tgt

    def qw(eng, sem, val=None):
        v = sem.n if val is None else val
        if v is None or v <= 0:
            return
        q(eng, lambda e, s=sem, v=v: e.wait_ge(s.h, v))

    def barrier():
        for eng in ENGINES:
            for s in allsems:
                qw(eng, s)

    def dram_in(name, shape, dt=F32):
        return nc.dram_tensor(name, list(shape), dt, kind="ExternalInput").ap()

    d_x = dram_in("x", (T, D))
    d_wqt = dram_in("wqt", (D, D))
    d_wkt = dram_in("wkt", (D, D))
    d_wvt = dram_in("wvt", (D, D))
    d_wot = dram_in("wot", (D, D))
    d_bo = dram_in("bo", (1, D))
    d_lq1 = dram_in("lq1", (1, DH))
    d_lq2 = dram_in("lq2", (1, DH))
    d_lk1 = dram_in("lk1", (1, DH))
    d_lk2 = dram_in("lk2", (1, DH))
    d_wa = dram_in("wa", (D, NEXP))
    d_rg = dram_in("rg", (NEXP, D, FF))
    d_rgb = dram_in("rgb", (NEXP, FF))
    d_rw1 = dram_in("rw1", (NEXP, D, FF))
    d_rb1 = dram_in("rb1", (NEXP, FF))
    d_rw2 = dram_in("rw2", (NEXP, FF, D))
    d_rb2 = dram_in("rb2", (NEXP, D))
    d_sg = dram_in("sg", (NSH, D, FF))
    d_sgb = dram_in("sgb", (NSH, FF))
    d_sw1 = dram_in("sw1", (NSH, D, FF))
    d_sb1 = dram_in("sb1", (NSH, FF))
    d_sw2 = dram_in("sw2", (NSH, FF, D))
    d_sb2 = dram_in("sb2", (NSH, D))
    d_i128b = dram_in("i128b", (128, 128), BF16)
    d_i128f = dram_in("i128f", (128, 128))
    d_i16f = dram_in("i16f", (16, 16))
    d_ind_m = dram_in("ind_m", (128, 1))
    d_ind_1m = dram_in("ind_1m", (128, 1))
    d_out = nc.dram_tensor("out", [T, D], F32, kind="ExternalOutput").ap()
    d_wsc = nc.dram_tensor("wscratch", [4, 16, 128, 8192], BF16, kind="Internal").ap()

    dbg_outs = {}

    def dbg(name, src, waits=()):
        if not _DEBUG:
            return
        ap = nc.dram_tensor("dbg_" + name, list(src.shape), src.dtype, kind="ExternalOutput").ap()
        dbg_outs[name] = ap
        for s, v in waits:
            qw("sync", s, v)
        qi("sync", lambda e, a=ap, sp=src: e.dma_start(out=a, in_=sp), st, 16)
        for engx in ("vector", "scalar", "tensor", "gpsimd", "sync"):
            qw(engx, st)

    ctx = ExitStack()
    sb = lambda name, shape, dt: ctx.enter_context(nc.sbuf_tensor(name, shape, dt))
    ps = lambda name, shape, dt: ctx.enter_context(nc.psum_tensor(name, shape, dt))

    xn = sb("xn", [128, NCH * MS], BF16)
    gbuf = sb("gbuf", [128, (MS // 128) * D], BF16)
    wring = sb("wring", [128, 2 * 8192], BF16)
    fbuf = sb("fbuf", [128, (MS // 128) * D], F32)
    ubuf = sb("ubuf", [128, FC * MS], BF16)
    utmp = sb("utmp", [128, 2 * 512], F32)

    def fb_bf16(c0, n):
        return fbuf[:, c0:c0 + n].bitcast(BF16)

    Q_ST = lambda: fb_bf16(0, 2048)
    K_ST = lambda: fb_bf16(2048, 2048)
    V_SB = lambda: fb_bf16(4096, 2048)
    O_ST = lambda: fb_bf16(6144, 2048)
    VSTA = lambda: fb_bf16(8192, 4096)
    VSTB = lambda: fb_bf16(12288, 4096)

    ubf32 = ubuf[:, :].bitcast(F32)
    e_f32 = ubf32[0:32, 0:512]
    eexp = ubf32[0:32, 512:1024]
    o_sb = ubf32[:, 1024:1536]
    o_sq = ubf32[:, 1536:2048]
    ebf = ubuf[0:32, 6144:6656]
    ebr = ubuf[0:32, 6656:7168]
    at1 = ubuf[0:32, 4096:5120]
    at2 = ubuf[0:32, 5120:6144]

    m_sb = sb("m_sb", [32, 2 * 16], F32)
    z_sb = sb("z_sb", [32, 2 * 16], F32)
    rz_sb = sb("rz_sb", [32, 2 * 16], F32)
    msq_t = utmp[0:1, 0:512]
    rstd1 = utmp[0:1, 512:1024]
    xslice = sb("xslice", [128, 2 * 512], F32)
    at1b = xslice[0:32, 0:512].bitcast(BF16)
    at2b = xslice[0:32, 512:1024].bitcast(BF16)
    outsl = sb("outsl", [128, 2 * 512], F32)
    xtile = fbuf[:, 0:2048]
    xnt = fbuf[:, 2048:3072].bitcast(BF16)
    wa_sb = sb("wa_sb", [128, NCH * NEXP], BF16)
    rgb_sb = fbuf[0:16, 4096:5120]
    rb1_sb = fbuf[0:16, 5120:6144]
    sgb_sb = fbuf[0:2, 6144:7168]
    sb1_sb = fbuf[0:2, 7168:8192]
    rgb_t = sb("rgb_t", [128, 128], F32)
    rb1_t = sb("rb1_t", [128, 128], F32)
    sgb_t = sb("sgb_t", [128, 16], F32)
    sb1_t = sb("sb1_t", [128, 16], F32)
    rb2_sb = sb("rb2_sb", [16, D], BF16)
    sb2_sb = sb("sb2_sb", [2, D], BF16)
    bo_sb = sb("bo_sb", [1, D], BF16)
    lrow = fbuf[0:1, 3072:3072 + 4 * DH]
    lscr = sb("lscr", [1, 8], F32)
    lam_sb = sb("lam_sb", [1, 1], F32)
    rowscale = sb("rowscale", [128, 1], F32)
    ind_m = sb("ind_m_s", [128, 1], F32)
    ind_1m = sb("ind_1m_s", [128, 1], F32)
    i128b = sb("i128b_s", [128, 128], BF16)
    i128f = sb("i128f_s", [128, 128], F32)
    i16f = sb("i16f_s", [16, 16], F32)
    ones1b = sb("ones1b", [1, 128], BF16)
    ones1f = sb("ones1f", [1, 128], F32)
    ones2b = sb("ones2b", [2, 128], BF16)
    onescf = sb("onescf", [128, 1], F32)
    cEPS = sb("cEPS", [128, 1], F32)
    cLN1M = sb("cLN1M", [1, 1], F32)
    aff_sb = sb("aff_sb", [128, 128], F32)
    top_sb = sb("top_sb", [128, 64], F32)
    den_sb = sb("den_sb", [128, 8], F32)
    rden_sb = sb("rden_sb", [128, 8], F32)
    mask_sb = sb("mask_sb", [128, 16], F32)
    cmb_sb = sb("cmb_sb", [128, 16], F32)
    ctm = sb("ctm", [128, 128], F32)
    cfm = sb("cfm", [16, MS], BF16)
    stats = sb("stats", [128, 8], F32)

    pA = ps("pA", [128, 512], F32)
    pB = ps("pB", [128, 512], F32)
    pC = ps("pC", [128, 512], F32)
    pD = ps("pD", [128, 512], F32)
    pS0 = ps("pS0", [128, 512], F32)
    pS1 = ps("pS1", [128, 512], F32)
    pO = ps("pO", [128, 512], F32)
    pX = ps("pX", [128, 512], F32)

    def pXb(i):
        return pX[:, :].bitcast(BF16)[:, i * 128:(i + 1) * 128]

    gates = {}

    def bank_gate(bank, eng="tensor"):
        g = gates.get(id(bank))
        if g:
            qw(eng, g[0], g[1])

    def bank_done(bank, sem):
        gates[id(bank)] = (sem, sem.n)

    ring = {"i": 0, "consumed": {}}
    swr = Sem("swr")
    allsems.append(swr)
    scstate = {}

    def wpiece(src_ap, a_dim, b_dim, cache=None):
        i = ring["i"]
        half = i % 2
        dst = wring[:, half * 8192: half * 8192 + a_dim * b_dim]
        dstv = dst.rearrange("p (a b) -> p a b", a=a_dim)
        prev = ring["consumed"].get(i - 2)
        if prev:
            qw("gpsimd", pe, prev)
        px = ring.get("extra", {}).get(i - 2)
        if px:
            qw("gpsimd", px[0], px[1])
        wsem = wv0 if half == 0 else wv1
        if cache is not None and scstate.get(cache):
            mi, pi = cache
            qw("gpsimd", swr, scstate[cache])
            tgt = qi("gpsimd", lambda e, d=dst, mi=mi, pi=pi: e.dma_start(
                out=d, in_=d_wsc[mi, pi, :, 0:d.shape[1]]), wsem, 16)
        else:
            tgt = qi("gpsimd", lambda e, d=dstv, s=src_ap: e.dma_start(out=d, in_=s), wsem, 16)
            if cache is not None:
                mi, pi = cache
                qw("sync", wsem, tgt)
                qw("sync", swr, swr.n)
                wt = qi("sync", lambda e, d=dst, mi=mi, pi=pi: e.dma_start(
                    out=d_wsc[mi, pi, :, 0:d.shape[1]], in_=d), swr, 16)
                scstate[cache] = wt
                ring.setdefault("extra", {})[i] = (swr, wt)
        ring["i"] = i + 1
        return dst, (wsem, tgt), i

    def wdone(i):
        ring["consumed"][i] = pe.n

    def sload(dst, src):
        return qi("sync", lambda e, d=dst, s=src: e.dma_start(out=d, in_=s), ld, 16)

    for dst, src in [
        (i128b[:, :], d_i128b), (i128f[:, :], d_i128f), (i16f[:, :], d_i16f),
        (ind_m[:, :], d_ind_m), (ind_1m[:, :], d_ind_1m),
        (rgb_sb[:, :], d_rgb), (rb1_sb[:, :], d_rb1),
        (sgb_sb[:, :], d_sgb), (sb1_sb[:, :], d_sb1),
        (lrow[0:1, 0:DH], d_lq1), (lrow[0:1, DH:2 * DH], d_lk1),
        (lrow[0:1, 2 * DH:3 * DH], d_lq2), (lrow[0:1, 3 * DH:4 * DH], d_lk2),
    ]:
        sload(dst, src)
    ld_consts = ld.n

    qi("gpsimd", lambda e: e.dma_start(out=rb2_sb[:, :], in_=d_rb2), wvb, 16)
    qi("gpsimd", lambda e: e.dma_start(out=sb2_sb[:, :], in_=d_sb2), wvb, 16)
    qi("gpsimd", lambda e: e.dma_start(out=bo_sb[:, :], in_=d_bo), wvb, 16)
    qi("gpsimd", lambda e: e.dma_start(
        out=wa_sb[:, :].rearrange("p (c e) -> p c e", c=NCH),
        in_=d_wa.rearrange("(c p) e -> p c e", p=128)), wvb, 16)
    wv_bias = wvb.n
    for buf in (ones1b, ones1f, ones2b, onescf):
        qi("gpsimd", lambda e, b=buf: e.memset(b[:, :], 1.0), gp, 1)
    qi("gpsimd", lambda e: e.memset(cEPS[:, :], EPS), gp, 1)
    qi("gpsimd", lambda e: e.memset(cLN1M[:, :], LN1M), gp, 1)
    qi("gpsimd", lambda e: e.memset(VSTA()[0:32, :], 0.0), gp, 1)
    qi("gpsimd", lambda e: e.memset(VSTB()[0:32, :], 0.0), gp, 1)
    gp_zero = gp.n

    qw("tensor", ld, ld_consts)
    for c in range(8):
        qi("tensor", lambda e, c=c: e.transpose(pX[:, c * 16:c * 16 + 16], rgb_sb[:, c * 128:(c + 1) * 128], i16f[:, :]), pe, 1)
        qi("tensor", lambda e, c=c: e.transpose(pX[:, 128 + c * 16:128 + c * 16 + 16], rb1_sb[:, c * 128:(c + 1) * 128], i16f[:, :]), pe, 1)
        qi("tensor", lambda e, c=c: e.transpose(pX[:, 256 + c * 2:256 + c * 2 + 2], sgb_sb[:, c * 128:(c + 1) * 128], i16f[0:2, 0:2]), pe, 1)
        qi("tensor", lambda e, c=c: e.transpose(pX[:, 272 + c * 2:272 + c * 2 + 2], sb1_sb[:, c * 128:(c + 1) * 128], i16f[0:2, 0:2]), pe, 1)
    qw("vector", pe)
    qi("vector", lambda e: e.tensor_copy(rgb_t[:, :], pX[:, 0:128]), dv, 1)
    qi("vector", lambda e: e.tensor_copy(rb1_t[:, :], pX[:, 128:256]), dv, 1)
    qi("vector", lambda e: e.tensor_copy(sgb_t[:, :], pX[:, 256:272]), dv, 1)
    qi("vector", lambda e: e.tensor_copy(sb1_t[:, :], pX[:, 272:288]), dv, 1)
    dv_bias = dv.n
    bank_done(pX, dv)

    qw("vector", ld, ld_consts)
    qi("vector", lambda e: e.tensor_mul(lrow[0:1, 0:DH], lrow[0:1, 0:DH], lrow[0:1, DH:2 * DH]), dv, 1)
    qi("vector", lambda e: e.reduce_sum(lscr[0:1, 0:1], lrow[0:1, 0:DH], axis=AX.X), dv, 1)
    qi("vector", lambda e: e.tensor_mul(lrow[0:1, 2 * DH:3 * DH], lrow[0:1, 2 * DH:3 * DH], lrow[0:1, 3 * DH:4 * DH]), dv, 1)
    qi("vector", lambda e: e.reduce_sum(lscr[0:1, 1:2], lrow[0:1, 2 * DH:3 * DH], axis=AX.X), dv, 1)
    qw("scalar", dv)
    qi("scalar", lambda e: e.activation(lscr[0:1, 2:3], lscr[0:1, 0:1], AF.Exp), ac, 1)
    qi("scalar", lambda e: e.activation(lscr[0:1, 3:4], lscr[0:1, 1:2], AF.Exp), ac, 1)
    qw("vector", ac)
    qi("vector", lambda e: e.tensor_sub(lscr[0:1, 4:5], lscr[0:1, 2:3], lscr[0:1, 3:4]), dv, 1)
    qi("vector", lambda e: e.tensor_scalar_add(lam_sb[0:1, 0:1], lscr[0:1, 4:5], LAM0), dv, 1)
    dv_lam = dv.n
    bank_gate(pX)
    qw("tensor", dv, dv_lam)
    qw("tensor", gp)
    qi("tensor", lambda e: e.matmul(pX[:, 0:1], ones1f[:, :], lam_sb[0:1, 0:1], start=True, stop=True), pe, 1)
    qw("vector", pe)
    qw("vector", ld, ld_consts)
    qi("vector", lambda e: e.tensor_scalar(rowscale[:, :], pX[:, 0:1], ind_m[:, :], ind_1m[:, :], ALU.mult, ALU.add), dv, 1)
    dv_rowscale = dv.n
    bank_done(pX, dv)

    slot_guard = {0: (0, 0), 1: (0, 0)}
    for ms in range(NMSLAB):
        ms_t0 = ms * MS
        NT = MS // 128
        barrier()

        prev_tile = {"dv_xnt": 0, "ac_sq": 0, "pe_tr": 0}
        for it in range(NT):
            t0 = ms_t0 + it * 128
            qw("sync", dv, prev_tile["dv_xnt"])
            qw("sync", ac, prev_tile["ac_sq"])
            sload(xtile[:, :], d_x[t0:t0 + 128, :])
            ld_x = ld.n
            qw("vector", ld, ld_x)
            qi("vector", lambda e: e.reduce_sum(stats[:, 0:1], xtile[:, :], axis=AX.X), dv, 1)
            qw("scalar", ld, ld_x)
            for s4 in range(4):
                qi("scalar", lambda e, s4=s4: e.activation(
                    o_sb[:, :], xtile[:, s4 * 512:(s4 + 1) * 512], AF.Square,
                    accum_out=stats[:, 1 + s4:2 + s4]), ac, 1)
            ac_sq = ac.n
            qi("vector", lambda e: e.tensor_scalar_mul(stats[:, 5:6], stats[:, 0:1], 1.0 / D), dv, 1)
            qw("vector", ac, ac_sq)
            qi("vector", lambda e: e.tensor_add(stats[:, 1:2], stats[:, 1:2], stats[:, 2:3]), dv, 1)
            qi("vector", lambda e: e.tensor_add(stats[:, 3:4], stats[:, 3:4], stats[:, 4:5]), dv, 1)
            qi("vector", lambda e: e.tensor_add(stats[:, 1:2], stats[:, 1:2], stats[:, 3:4]), dv, 1)
            qi("vector", lambda e: e.tensor_mul(stats[:, 6:7], stats[:, 5:6], stats[:, 5:6]), dv, 1)
            qi("vector", lambda e: e.tensor_scalar_mul(stats[:, 7:8], stats[:, 1:2], 1.0 / D), dv, 1)
            qi("vector", lambda e: e.tensor_sub(stats[:, 7:8], stats[:, 7:8], stats[:, 6:7]), dv, 1)
            dv_var = dv.n
            qw("scalar", dv, dv_var)
            qi("scalar", lambda e: e.activation(stats[:, 1:2], stats[:, 7:8], AF.Ln, bias=cEPS[:, :]), ac, 1)
            qi("scalar", lambda e: e.activation(stats[:, 2:3], stats[:, 1:2], AF.Exp, scale=-0.5), ac, 1)
            ac_rstd = ac.n
            qw("vector", ac, ac_rstd)
            qw("vector", pe, prev_tile["pe_tr"])
            qi("vector", lambda e: e.tensor_scalar(xnt[:, :], xtile[:, :], stats[:, 5:6], stats[:, 2:3], ALU.subtract, ALU.mult), dv, 1)
            dv_xnt = dv.n
            bank_gate(pX)
            qw("tensor", dv, dv_xnt)
            for half in range(2):
                for cc in range(8):
                    c = half * 8 + cc
                    qi("tensor", lambda e, c=c, cc=cc: e.transpose(pXb(cc), xnt[:, c * 128:(c + 1) * 128], i128b[:, :]), pe, 1)
                pe_tr = pe.n
                qw("vector", pe, pe_tr)
                for cc in range(8):
                    c = half * 8 + cc
                    qi("vector", lambda e, c=c, cc=cc, it=it: e.tensor_copy(
                        xn[:, c * MS + it * 128: c * MS + it * 128 + 128], pXb(cc)), dv, 1)
                if half == 0:
                    qw("tensor", dv)
            bank_done(pX, dv)
            prev_tile = {"dv_xnt": dv_xnt, "ac_sq": ac_sq, "pe_tr": pe.n}
        dv_xn = dv.n
        dbg("xn_ms%d" % ms, xn[:, :], waits=[(dv, dv_xn)])

        bank_gate(pC)
        qw("tensor", dv, dv_xn)
        qw("tensor", wvb, wv_bias)
        for i in range(8):
            for kc in range(NCH):
                qi("tensor", lambda e, i=i, kc=kc: e.matmul(
                    pC[:, i * 16:(i + 1) * 16],
                    xn[:, kc * MS + i * 128: kc * MS + (i + 1) * 128],
                    wa_sb[:, kc * 16:(kc + 1) * 16],
                    start=(kc == 0), stop=(kc == NCH - 1)),
                    pe, 1 if kc == NCH - 1 else 0)
        pe_aff = pe.n
        qw("scalar", pe, pe_aff)
        qi("scalar", lambda e: e.activation(aff_sb[:, :], pC[:, 0:128], AF.Sigmoid), ac, 1)
        ac_aff = ac.n
        bank_done(pC, ac)
        qw("vector", ac, ac_aff)
        for i in range(8):
            qi("vector", lambda e, i=i: e.max(top_sb[:, i * 8:(i + 1) * 8], aff_sb[:, i * 16:(i + 1) * 16]), dv, 1)
        for i in range(8):
            qi("vector", lambda e, i=i: e.tensor_add(den_sb[:, i:i + 1], top_sb[:, i * 8:i * 8 + 1], top_sb[:, i * 8 + 1:i * 8 + 2]), dv, 1)
        qi("vector", lambda e: e.reciprocal(rden_sb[:, :], den_sb[:, :]), dv, 1)
        dv_top = dv.n
        qw("vector", dv, dv_top)
        for i in range(8):
            qi("vector", lambda e, i=i: e.tensor_single_scalar(mask_sb[:, :], aff_sb[:, i * 16:(i + 1) * 16], top_sb[:, i * 8 + 1:i * 8 + 2], ALU.is_ge), dv, 1)
            qi("vector", lambda e, i=i: e.tensor_mul(cmb_sb[:, :], aff_sb[:, i * 16:(i + 1) * 16], mask_sb[:, :]), dv, 1)
            qi("vector", lambda e, i=i: e.tensor_scalar_mul(ctm[:, i * 16:(i + 1) * 16], cmb_sb[:, :], rden_sb[:, i:i + 1]), dv, 1)
        dv_ctm = dv.n
        dbg("aff_ms%d" % ms, aff_sb[:, :], waits=[(ac, ac_aff)])
        dbg("top_ms%d" % ms, top_sb[:, :], waits=[(dv, dv_top)])
        dbg("ctm_ms%d" % ms, ctm[:, :], waits=[(dv, dv_ctm)])
        for rnd in range(2):
            bank_gate(pX)
            qw("tensor", dv, dv_ctm)
            for i4 in range(4):
                i = rnd * 4 + i4
                qi("tensor", lambda e, i=i, i4=i4: e.transpose(pX[0:16, i4 * 128:(i4 + 1) * 128], ctm[:, i * 16:(i + 1) * 16], i128f[:, :]), pe, 1)
            pe_cfm = pe.n
            qw("vector", pe, pe_cfm)
            for i4 in range(4):
                i = rnd * 4 + i4
                qi("vector", lambda e, i=i, i4=i4: e.tensor_copy(cfm[0:16, i * 128:(i + 1) * 128], pX[0:16, i4 * 128:(i4 + 1) * 128]), dv, 1)
            bank_done(pX, dv)
        dv_cfm = dv.n

        qw("tensor", dv, dv_cfm)
        qw("tensor", wvb, wv_bias)
        for sub in range(8):
            for dblk in range(4):
                bank = [pC, pD][(sub * 4 + dblk) % 2]
                bank_gate(bank)
                qi("tensor", lambda e, b=bank, sub=sub, dblk=dblk: e.matmul(
                    b[:, :], cfm[0:16, sub * 128:(sub + 1) * 128],
                    rb2_sb[:, dblk * 512:(dblk + 1) * 512], start=True, stop=False), pe, 0)
                qi("tensor", lambda e, b=bank, dblk=dblk: e.matmul(
                    b[:, :], ones2b[:, :], sb2_sb[:, dblk * 512:(dblk + 1) * 512],
                    start=False, stop=False), pe, 0)
                for j in range(4):
                    c = dblk * 4 + j
                    qi("tensor", lambda e, b=bank, c=c, j=j, sub=sub: e.matmul(
                        b[:, j * 128:(j + 1) * 128],
                        xn[:, c * MS + sub * 128: c * MS + (sub + 1) * 128],
                        i128b[:, :], start=False, stop=(j == 3)),
                        pe, 1 if j == 3 else 0)
                pe_fi = pe.n
                qw("vector", pe, pe_fi)
                qi("vector", lambda e, b=bank, sub=sub, dblk=dblk: e.tensor_copy(
                    fbuf[:, sub * 2048 + dblk * 512: sub * 2048 + (dblk + 1) * 512], b[:, :]), dv, 1)
                bank_done(bank, dv)
        qw("vector", dv, dv_ctm)
        qw("vector", dv, dv_bias)
        qw("scalar", dv, dv_bias)

        prev_dv_u = {0: 0, 1: 0}
        for ex in range(NEXP + NSH):
            shared = ex >= NEXP
            jj = ex - NEXP
            if shared:
                dgm, dw1m, dw2m = d_sg[jj], d_sw1[jj], d_sw2[jj]
                gbias = lambda c, jj=jj: sgb_t[:, c * 2 + jj:c * 2 + jj + 1]
                bbias = lambda c, jj=jj: sb1_t[:, c * 2 + jj:c * 2 + jj + 1]
            else:
                dgm, dw1m, dw2m = d_rg[ex], d_rw1[ex], d_rw2[ex]
                gbias = lambda c, ex=ex: rgb_t[:, c * 16 + ex:c * 16 + ex + 1]
                bbias = lambda c, ex=ex: rb1_t[:, c * 16 + ex:c * 16 + ex + 1]

            for fh in range(2):
                gsrc = dgm[:, fh * 512:(fh + 1) * 512].rearrange("(c p) f -> p c f", p=128)
                w1src = dw1m[:, fh * 512:(fh + 1) * 512].rearrange("(c p) f -> p c f", p=128)
                gdst, gw, gi_ = wpiece(gsrc, NCH, 512)
                w1dst, w1w, w1i = wpiece(w1src, NCH, 512)
                for fcl in range(4):
                    fc = fh * 4 + fcl
                    if fcl % 2 == 0:
                        Ab, Bb = [pA, pB], [pC, pD]
                    else:
                        Ab, Bb = [pS0, pS1], [pO, pX]
                    qw("tensor", gw[0], gw[1])
                    for ss in range(2):
                        bank_gate(Ab[ss])
                        for kc in range(NCH):
                            qi("tensor", lambda e, b=Ab[ss], g=gdst, kc=kc, fcl=fcl, ss=ss: e.matmul(
                                b[:, :],
                                g.rearrange("p (c f) -> p c f", c=NCH)[:, kc, fcl * 128:(fcl + 1) * 128],
                                xn[:, kc * MS + ss * 512: kc * MS + ss * 512 + 512],
                                start=(kc == 0), stop=(kc == NCH - 1)),
                                pe, 1 if kc == NCH - 1 else 0)
                    qw("tensor", w1w[0], w1w[1])
                    for ss in range(2):
                        bank_gate(Bb[ss])
                        for kc in range(NCH):
                            qi("tensor", lambda e, b=Bb[ss], g=w1dst, kc=kc, fcl=fcl, ss=ss: e.matmul(
                                b[:, :],
                                g.rearrange("p (c f) -> p c f", c=NCH)[:, kc, fcl * 128:(fcl + 1) * 128],
                                xn[:, kc * MS + ss * 512: kc * MS + ss * 512 + 512],
                                start=(kc == 0), stop=(kc == NCH - 1)),
                                pe, 1 if kc == NCH - 1 else 0)
                    pe_ab = pe.n
                    for ss in range(2):
                        qw("scalar", pe, pe_ab)
                        qw("scalar", dv, prev_dv_u[ss])
                        qi("scalar", lambda e, b=Ab[ss], ss=ss, fc=fc: e.activation(
                            utmp[:, ss * 512:(ss + 1) * 512], b[:, :], AF.Gelu, bias=gbias(fc)), ac, 1)
                    ac_g = ac.n
                    qw("vector", ac, ac_g)
                    for ss in range(2):
                        tgt = qi("vector", lambda e, bB=Bb[ss], ss=ss, fc=fc: e.scalar_tensor_tensor(
                            ubuf[:, fc * MS + ss * 512: fc * MS + ss * 512 + 512],
                            bB[:, :], bbias(fc), utmp[:, ss * 512:(ss + 1) * 512],
                            ALU.add, ALU.mult), dv, 1)
                        prev_dv_u[ss] = tgt
                    for ss in range(2):
                        bank_done(Ab[ss], ac)
                        bank_done(Bb[ss], dv)
                wdone(gi_)
                wdone(w1i)

            dv_u_all = dv.n
            for dh in range(2):
                w2src = dw2m[:, dh * 1024:(dh + 1) * 1024].rearrange("(c p) d -> p c d", p=128)
                w2dst, w2w, w2i = wpiece(w2src, FC, 1024)
                qw("tensor", w2w[0], w2w[1])
                qw("tensor", dv, dv_u_all)
                for sub in range(8):
                    for db in range(2):
                        dblk = dh * 2 + db
                        bank = [pA, pC, pS0, pO][(sub * 2 + db) % 4]
                        bank_gate(bank)
                        for kc in range(FC):
                            qi("tensor", lambda e, b=bank, kc=kc, sub=sub, db=db, w2=w2dst: e.matmul(
                                b[:, :],
                                ubuf[:, kc * MS + sub * 128: kc * MS + (sub + 1) * 128],
                                w2.rearrange("p (c d) -> p c d", c=FC)[:, kc, db * 512:(db + 1) * 512],
                                start=(kc == 0), stop=(kc == FC - 1)),
                                pe, 1 if kc == FC - 1 else 0)
                        pe_w2 = pe.n
                        qw("vector", pe, pe_w2)
                        if shared:
                            qi("vector", lambda e, b=bank, sub=sub, dblk=dblk: e.tensor_add(
                                fbuf[:, sub * 2048 + dblk * 512: sub * 2048 + (dblk + 1) * 512],
                                b[:, :],
                                fbuf[:, sub * 2048 + dblk * 512: sub * 2048 + (dblk + 1) * 512]), dv, 1)
                        else:
                            qi("vector", lambda e, b=bank, sub=sub, dblk=dblk, ex=ex: e.scalar_tensor_tensor(
                                fbuf[:, sub * 2048 + dblk * 512: sub * 2048 + (dblk + 1) * 512],
                                b[:, :], ctm[:, sub * 16 + ex:sub * 16 + ex + 1],
                                fbuf[:, sub * 2048 + dblk * 512: sub * 2048 + (dblk + 1) * 512],
                                ALU.mult, ALU.add), dv, 1)
                        bank_done(bank, dv)
                wdone(w2i)
        dv_f = dv.n
        dbg("f_ms%d" % ms, fbuf[:, 0:2048], waits=[(dv, dv_f)])

        qw("scalar", dv, dv_f)
        ac_silu = []
        for it in range(NT):
            ac_silu.append(qi("scalar", lambda e, it=it: e.activation(
                gbuf[:, it * 2048:(it + 1) * 2048], fbuf[:, it * 2048:(it + 1) * 2048], AF.Silu), ac, 1))
        ac_g_all = ac.n
        dbg("g_ms%d" % ms, gbuf[:, 0:2048], waits=[(ac, ac_g_all)])

        qw("gpsimd", ac, ac_silu[7])
        qw("gpsimd", pe)
        qw("gpsimd", dv)
        qi("gpsimd", lambda e: e.memset(VSTA()[0:32, :], 0.0), gp, 1)
        qi("gpsimd", lambda e: e.memset(VSTB()[0:32, :], 0.0), gp, 1)
        gp_zero = gp.n

        for sl in range(NSLAB):
            st0 = ms_t0 + sl * SLAB
            ls0 = sl * SLAB
            qst, kst, vsb, ost = Q_ST(), K_ST(), V_SB(), O_ST()
            vsts = [VSTA(), VSTB()]

            for mi, (dmat, kind) in enumerate([(d_wqt, "q"), (d_wkt, "k"), (d_wvt, "v")]):
                for quarter in range(4):
                    src = dmat[:, quarter * 512:(quarter + 1) * 512].rearrange("(c p) f -> p c f", p=128)
                    wdst, ww, wi = wpiece(src, NCH, 512, cache=(mi, quarter))
                    qw("tensor", ww[0], ww[1])
                    for jjq in range(4):
                        ochunk = quarter * 4 + jjq
                        bank = [pA, pB, pC, pD][ochunk % 4]
                        bank_gate(bank)
                        for kc in range(NCH):
                            qi("tensor", lambda e, b=bank, w=wdst, kc=kc, jjq=jjq, ls0=ls0: e.matmul(
                                b[:, 0:SLAB],
                                w.rearrange("p (c f) -> p c f", c=NCH)[:, kc, jjq * 128:(jjq + 1) * 128],
                                xn[:, kc * MS + ls0: kc * MS + ls0 + SLAB],
                                start=(kc == 0), stop=(kc == NCH - 1)),
                                pe, 1 if kc == NCH - 1 else 0)
                        pe_pr = pe.n
                        qw("vector", pe, pe_pr)
                        qw("vector", ac, ac_silu[2] if kind == "v" else (ac_silu[0] if kind == "q" else ac_silu[1]))
                        if kind == "v":
                            qi("vector", lambda e, b=bank, oc=ochunk: e.tensor_copy(
                                vsb[:, oc * SLAB:(oc + 1) * SLAB], b[:, 0:SLAB]), dv, 1)
                        else:
                            dst = qst if kind == "q" else kst
                            scale = 0.125 if kind == "q" else 1.0
                            for eps_ in range(2):
                                a_idx = 2 * ochunk + eps_
                                for r in range(2):
                                    qi("vector", lambda e, b=bank, dstv=dst, eps_=eps_, r=r, a=a_idx, sc=scale: e.tensor_scalar_mul(
                                        dstv[r * 64:(r + 1) * 64, a * 128:(a + 1) * 128],
                                        b[eps_ * 64:(eps_ + 1) * 64, r * 128:(r + 1) * 128], sc), dv, 1)
                        bank_done(bank, dv)
                    wdone(wi)
            dv_proj = dv.n
            if sl == 0:
                dbg("qst_ms%d" % ms, qst[:, :], waits=[(dv, dv_proj)])
                dbg("kst_ms%d" % ms, kst[:, :], waits=[(dv, dv_proj)])
                dbg("vsb_ms%d" % ms, vsb[:, :], waits=[(dv, dv_proj)])

            vd_t = {}

            def v_scatter(g, buf, bi_, pe_guard):
                qw("sync", dv, dv_proj)
                qw("sync", gp, gp_zero)
                if pe_guard:
                    qw("sync", pe, pe_guard)
                vs = vda[bi_]
                for h in range(16):
                    qi("sync", lambda e, h=h, g=g, b=buf: e.dma_start(
                        out=b[h:h + 1, :].rearrange("r (ee t) -> r ee t", ee=128),
                        in_=vsb[:, h * SLAB + g * 64: h * SLAB + (g + 1) * 64]), vs, 16)
                vd_t[g] = (vs, vs.n)

            v_scatter(0, vsts[0], 0, 0)
            v_scatter(2, vsts[1], 1, 0)

            if _DEBUG and ms == 0 and sl == 0:
                qw("sync", vda[0], vda[0].n)
                qw("sync", vda[1], vda[1].n)
                dbg("vsta_s0", VSTA()[0:32, :], waits=[])
                dbg("vstb_s0", VSTB()[0:32, :], waits=[])
            sbgrp = [[pS0, pS1], [pB, pC]]
            opair = [pO, pA]
            obk = [pX, pD]
            at_set = [(at1, at2), (at1b, at2b)]
            pe_s_rec = {}
            dv_at = {}
            pe_diffv = {-1: 0, -2: 0}
            st_o = {}

            def emit_S(p):
                for r in range(2):
                    sbank = sbgrp[p % 2][r]
                    bank_gate(sbank)
                    qw("tensor", dv, dv_proj)
                    for sl_tok in range(16):
                        ur = p * 16 + sl_tok
                        qi("tensor", lambda e, sb_=sbank, r=r, ur=ur, sl_tok=sl_tok: e.matmul(
                            sb_[0:32, sl_tok * 32:(sl_tok + 1) * 32],
                            qst[r * 64:(r + 1) * 64, :].rearrange("p (a t) -> p t a", t=128)[:, ur, :],
                            kst[r * 64:(r + 1) * 64, :].rearrange("p (a t) -> p t a", t=128)[:, ur, :],
                            start=True, stop=True),
                            pe, 1 if sl_tok == 15 else 0)
                pe_s_rec[p] = pe.n

            emit_S(0)
            for p in range(9):
                if p < 8:
                    a1, a2 = at_set[p % 2]
                    for r in range(2):
                        sbank = sbgrp[p % 2][r]
                        qw("vector", pe, pe_s_rec[p])
                        qw("vector", pe, pe_diffv[p - 2])
                        qi("vector", lambda e, sb_=sbank, r=r: e.reduce_max(
                            m_sb[:, r * 16:(r + 1) * 16],
                            sb_[0:32, :].rearrange("p (t b) -> p t b", b=32), axis=AX.X), dv, 1)
                        qi("vector", lambda e, sb_=sbank, r=r: e.tensor_tensor(
                            eexp[:, :].rearrange("p (t b) -> p t b", b=32),
                            sb_[0:32, :].rearrange("p (t b) -> p t b", b=32),
                            m_sb[:, r * 16:(r + 1) * 16].unsqueeze(2).to_broadcast((32, 16, 32)),
                            ALU.subtract), dv, 1)
                        dv_e = dv.n
                        bank_done(sbank, dv)
                        qw("scalar", dv, dv_e)
                        qi("scalar", lambda e: e.activation(eexp[:, :], eexp[:, :], AF.Exp), ac, 1)
                        ac_e = ac.n
                        qw("vector", ac, ac_e)
                        qi("vector", lambda e, r=r: e.reduce_sum(
                            z_sb[:, r * 16:(r + 1) * 16],
                            eexp[:, :].rearrange("p (t b) -> p t b", b=32), axis=AX.X), dv, 1)
                        qi("vector", lambda e, r=r: e.reciprocal(rz_sb[:, r * 16:(r + 1) * 16], z_sb[:, r * 16:(r + 1) * 16]), dv, 1)
                        qw("vector", dv, dv_rowscale)
                        qi("vector", lambda e, r=r: e.scalar_tensor_tensor(
                            ebf[0:32, :].rearrange("p (t b) -> p t b", b=32),
                            eexp[:, :].rearrange("p (t b) -> p t b", b=32),
                            rowscale[0:32, :],
                            rz_sb[:, r * 16:(r + 1) * 16].unsqueeze(2).to_broadcast((32, 16, 32)),
                            ALU.mult, ALU.mult), dv, 1)
                        qi("vector", lambda e, a1=a1, r=r: e.transpose(a1[:, r * 512:(r + 1) * 512], ebf[0:32, :]), dv, 1)
                        qi("vector", lambda e: e.tensor_copy(
                            ebr[:, :].rearrange("p (t h i) -> p t h i", t=16, h=2),
                            bass.AP(ubuf, 6144 + 16, [[8192, 32], [32, 16], [-16, 2], [1, 16]])), dv, 1)
                        qi("vector", lambda e, a2=a2, r=r: e.transpose(
                            a2[:, r * 512:(r + 1) * 512], ebr[:, :]), dv, 1)
                        dv_at[(p, r)] = dv.n
                    emit_S(p + 1) if p + 1 < 8 else None
                    if p == 4:
                        v_scatter(1, vsts[0], 0, pe_diffv[3])
                        v_scatter(3, vsts[1], 1, pe_diffv[3])
                if p >= 1:
                    pm = p - 1
                    bcb = obk[pm % 2]
                    qw("vector", pe, st_o[pm]["pe_bc"])
                    qw("vector", ac, ac_silu[3])
                    qi("vector", lambda e, pm=pm, ostv=ost, bcb=bcb: e.tensor_mul(
                        ostv.rearrange("q (a r i) -> q a r i", a=16, r=2)[:, :, :, pm * 16:(pm + 1) * 16],
                        o_sb[:, :].rearrange("q (r i a) -> q a r i", r=2, i=16),
                        bcb[:, :].rearrange("q (r i a) -> q a r i", r=2, i=16)), dv, 1)
                    bank_done(bcb, dv)
                if p < 8:
                    obank = opair[p % 2]
                    bank_gate(obank)
                    for r in range(2):
                        qw("tensor", dv, dv_at[(p, r)])
                        gq = (0 if p < 4 else 1) if r == 0 else (2 if p < 4 else 3)
                        vbuf = vsts[r]
                        qw("tensor", vd_t[gq][0], vd_t[gq][1])
                        a1, a2 = at_set[p % 2]
                        for sl_tok in range(16):
                            u = r * 128 + p * 16 + sl_tok
                            ugl = u % 64
                            pairtok = r * 16 + sl_tok
                            qi("tensor", lambda e, ob=obank, vb=vbuf, ugl=ugl, a1=a1, r=r, sl_tok=sl_tok, pairtok=pairtok: e.matmul(
                                ob[:, pairtok * 16:(pairtok + 1) * 16],
                                vb[0:32, :].rearrange("p (ee t) -> p t ee", t=64)[:, ugl, :],
                                a1[:, r * 512 + sl_tok * 32: r * 512 + sl_tok * 32 + 16],
                                start=True, stop=False), pe, 0)
                            qi("tensor", lambda e, ob=obank, vb=vbuf, ugl=ugl, a2=a2, r=r, sl_tok=sl_tok, pairtok=pairtok: e.matmul(
                                ob[:, pairtok * 16:(pairtok + 1) * 16],
                                vb[0:32, :].rearrange("p (ee t) -> p t ee", t=64)[:, ugl, :],
                                a2[:, r * 512 + sl_tok * 32 + 16: r * 512 + sl_tok * 32 + 32],
                                start=False, stop=True),
                                pe, 1 if sl_tok == 15 else 0)
                    pe_diffv[p] = pe.n
                    qw("vector", pe, pe_diffv[p])
                    qi("vector", lambda e, ob=obank: e.tensor_copy(o_sb[:, :], ob[:, :]), dv, 1)
                    dv_ocp = dv.n
                    if _DEBUG and ms == 0 and sl == 0 and p == 0:
                        dbg("at1_p0", at1[:, :], waits=[(dv, dv_at[(0, 1)])])
                        dbg("at2_p0", at2[:, :], waits=[(dv, dv_at[(0, 1)])])
                        dbg("osb_p0", o_sb[:, :], waits=[(dv, dv_ocp)])
                    bank_done(obank, dv)
                    qw("scalar", dv, dv_ocp)
                    if p >= 1:
                        qw("scalar", pe, st_o[p - 1]["pe_msq"])
                    qi("scalar", lambda e: e.activation(o_sq[:, :], o_sb[:, :], AF.Square), ac, 1)
                    ac_osq = ac.n
                    mb = obk[p % 2]
                    bank_gate(mb)
                    qw("tensor", ac, ac_osq)
                    qi("tensor", lambda e, mb=mb: e.matmul(mb[0:1, :], onescf[:, :], o_sq[:, :], start=True, stop=True), pe, 1)
                    pe_msq = pe.n
                    qw("scalar", pe, pe_msq)
                    qi("scalar", lambda e, mb=mb: e.activation(msq_t[0:1, :], mb[0:1, :], AF.Ln, bias=cEPS[0:1, :], scale=1.0 / EH), ac, 1)
                    qi("scalar", lambda e: e.activation(rstd1[0:1, :], msq_t[0:1, :], AF.Exp, scale=-0.5, bias=cLN1M[0:1, :]), ac, 1)
                    ac_rst = ac.n
                    qw("tensor", ac, ac_rst)
                    qi("tensor", lambda e, mb=mb: e.matmul(mb[:, :], ones1f[:, :], rstd1[0:1, :], start=True, stop=True), pe, 1)
                    st_o[p] = {"pe_msq": pe_msq, "pe_bc": pe.n}
            dv_att_done = dv.n
            if sl == 0:
                dbg("ost_ms%d" % ms, ost[:, :], waits=[(dv, dv_att_done)])

            qw("sync", pe)
            for dq in range(4):
                src = d_wot[:, dq * 512:(dq + 1) * 512].rearrange("(a p) d -> p a d", p=128)
                wdst, ww, wi = wpiece(src, 16, 512, cache=(3, dq))
                qw("tensor", ww[0], ww[1])
                for tsub in range(2):
                    t0g = st0 + tsub * 128
                    slot = tsub
                    qw("sync", dv, slot_guard[slot][0])
                    qi("sync", lambda e, t0g=t0g, dq=dq, slot=slot: e.dma_start(
                        out=xslice[:, slot * 512:(slot + 1) * 512],
                        in_=d_x[t0g:t0g + 128, dq * 512:(dq + 1) * 512]), lds[slot], 16)
                    ld_xs = lds[slot].n
                    bank = [pB, pC][tsub]
                    bank_gate(bank)
                    qw("tensor", dv, dv_att_done)
                    for a in range(16):
                        qi("tensor", lambda e, b=bank, a=a, tsub=tsub, w=wdst, ostv=ost: e.matmul(
                            b[:, :],
                            ostv[:, a * SLAB + tsub * 128: a * SLAB + tsub * 128 + 128],
                            w.rearrange("p (a d) -> p a d", a=16)[:, a, :],
                            start=(a == 0), stop=False), pe, 0)
                    qi("tensor", lambda e, b=bank, dq=dq: e.matmul(
                        b[:, :], ones1b[:, :], bo_sb[0:1, dq * 512:(dq + 1) * 512],
                        start=False, stop=True), pe, 1)
                    pe_wo = pe.n
                    qw("vector", pe, pe_wo)
                    qw("vector", lds[slot], ld_xs)
                    qw("vector", sts[slot], slot_guard[slot][1])
                    gi = sl * 2 + tsub
                    qw("vector", ac, ac_silu[gi])
                    qi("vector", lambda e, b=bank, gi=gi, dq=dq, slot=slot: e.tensor_mul(
                        outsl[:, slot * 512:(slot + 1) * 512],
                        b[:, :], gbuf[:, gi * 2048 + dq * 512: gi * 2048 + (dq + 1) * 512]), dv, 1)
                    dv_out = qi("vector", lambda e, slot=slot: e.tensor_add(
                        outsl[:, slot * 512:(slot + 1) * 512],
                        outsl[:, slot * 512:(slot + 1) * 512],
                        xslice[:, slot * 512:(slot + 1) * 512]), dv, 1)
                    bank_done(bank, dv)
                    qw("sync", dv, dv_out)
                    st_tgt = qi("sync", lambda e, t0g=t0g, dq=dq, slot=slot: e.dma_start(
                        out=d_out[t0g:t0g + 128, dq * 512:(dq + 1) * 512],
                        in_=outsl[:, slot * 512:(slot + 1) * 512]), sts[slot], 16)
                    slot_guard[slot] = (dv_out, st_tgt)
                wdone(wi)

    qw("sync", st)
    qw("sync", sts[0])
    qw("sync", sts[1])
    qw("sync", vda[0])
    qw("sync", vda[1])

    semctx = ExitStack()
    for s in allsems:
        s.h = semctx.enter_context(nc.semaphore("sem_" + s.name))
    with nc.Block() as blk:
        blk.sync(lambda e: [f(e) for f in Q["sync"]])
        blk.gpsimd(lambda e: [f(e) for f in Q["gpsimd"]])
        blk.vector(lambda e: [f(e) for f in Q["vector"]])
        blk.scalar(lambda e: [f(e) for f in Q["scalar"]])
        blk.tensor(lambda e: [f(e) for f in Q["tensor"]])
    semctx.close()
    ctx.close()
    return nc, dbg_outs


def _host_inputs(inputs):
    f32 = np.float32
    g = lambda k: np.asarray(inputs[k], dtype=f32)
    perm = np.zeros(D, dtype=np.int64)
    for a in range(32):
        c, h = divmod(a, 16)
        for d in range(DH):
            perm[a * DH + d] = h * 128 + 2 * d + c
    shared = {
        "wqt": np.ascontiguousarray(g("L0_attn_Wq")[perm, :].T),
        "wkt": np.ascontiguousarray(g("L0_attn_Wk")[perm, :].T),
        "wvt": np.ascontiguousarray(g("L0_attn_Wv").T),
        "wot": np.ascontiguousarray(g("L0_attn_Wo").T),
        "bo": np.ascontiguousarray(g("L0_attn_bo").reshape(1, D)),
        "lq1": np.ascontiguousarray(g("L0_attn_lq1").reshape(1, DH)),
        "lq2": np.ascontiguousarray(g("L0_attn_lq2").reshape(1, DH)),
        "lk1": np.ascontiguousarray(g("L0_attn_lk1").reshape(1, DH)),
        "lk2": np.ascontiguousarray(g("L0_attn_lk2").reshape(1, DH)),
        "wa": np.ascontiguousarray(g("L0_ffn_wa")),
        "rg": np.ascontiguousarray(g("L0_ffn_rg")),
        "rgb": np.ascontiguousarray(g("L0_ffn_rgb")),
        "rw1": np.ascontiguousarray(g("L0_ffn_rw1")),
        "rb1": np.ascontiguousarray(g("L0_ffn_rb1")),
        "rw2": np.ascontiguousarray(g("L0_ffn_rw2")),
        "rb2": np.ascontiguousarray(g("L0_ffn_rb2")),
        "sg": np.ascontiguousarray(g("L0_ffn_sg")),
        "sgb": np.ascontiguousarray(g("L0_ffn_sgb")),
        "sw1": np.ascontiguousarray(g("L0_ffn_sw1")),
        "sb1": np.ascontiguousarray(g("L0_ffn_sb1")),
        "sw2": np.ascontiguousarray(g("L0_ffn_sw2")),
        "sb2": np.ascontiguousarray(g("L0_ffn_sb2")),
        "i128b": np.eye(128).astype(ml_dtypes.bfloat16),
        "i128f": np.eye(128, dtype=f32),
        "i16f": np.eye(16, dtype=f32),
        "ind_m": (-((np.arange(128) % 32) >= 16).astype(f32)).reshape(128, 1),
        "ind_1m": (1.0 - ((np.arange(128) % 32) >= 16).astype(f32)).reshape(128, 1),
    }
    return shared


_LAST_RESULTS = None


def kernel(**inputs):
    global _LAST_RESULTS
    x = np.asarray(inputs["L0_attn_x"], dtype=np.float32).reshape(16384, D)
    shared = _host_inputs(inputs)
    nc, dbg_outs = build_program()
    in_maps = []
    for c in range(NCORES):
        m = dict(shared)
        m["x"] = np.ascontiguousarray(x[c * T:(c + 1) * T, :])
        in_maps.append(m)
    res = run_bass_kernel_spmd(nc, in_maps, core_ids=list(range(NCORES)))
    _LAST_RESULTS = res.results
    out = np.concatenate([res.results[c]["out"] for c in range(NCORES)], axis=0)
    return out.reshape(4, 4096, D)


# version 06
# speedup vs baseline: 1.2876x; 1.2876x over previous
from contextlib import ExitStack

import numpy as np
import ml_dtypes

import concourse.bass as bass
import concourse.mybir as mybir
from concourse.bass_utils import run_bass_kernel_spmd

F32 = mybir.dt.float32
BF16 = mybir.dt.bfloat16
AF = mybir.ActivationFunctionType
ALU = mybir.AluOpType
AX = mybir.AxisListType

_DEBUG = False
_SIM_SAFE = True

NCORES = 8
T = 2048
MS = 1024
NMSLAB = T // MS
SLAB = 256
NSLAB = MS // SLAB
D = 2048
NCH = D // 128
FF = 1024
FC = FF // 128
NEXP = 16
NSH = 2
H = 16
DH = 64
EH = 128
LAM0 = 0.8 - 0.6 * float(np.exp(-0.3 * 12))
LN1M = float(np.log(1.0 - LAM0))
EPS = 1e-5

ENGINES = ("sync", "gpsimd", "vector", "scalar", "tensor")


class Sem:
    def __init__(self, name):
        self.name = name
        self.n = 0
        self.h = None


def build_program():
    nc = bass.Bass("TRN2", target_bir_lowering=False)
    Q = {e: [] for e in ENGINES}
    ld, wv, vd, st, pe, dv, ac, gp = (Sem(s) for s in ("ld", "wv", "vd", "st", "pe", "dv", "ac", "gp"))
    wvb = Sem("wvb"); wv0 = Sem("wv0"); wv1 = Sem("wv1"); wv2 = Sem("wv2"); wv3 = Sem("wv3")
    lds = [Sem("lds0"), Sem("lds1")]
    sts = [Sem("sts0"), Sem("sts1")]
    vda = [Sem("vda"), Sem("vdb")]
    allsems = [ld, wv, vd, st, pe, dv, ac, gp, wvb, wv0, wv1, wv2, wv3] + lds + sts + vda

    def q(eng, fn):
        Q[eng].append(fn)

    eng_sem = {"vector": dv, "scalar": ac}

    def qi(eng, make, sem=None, k=1):
        if _SIM_SAFE and eng in eng_sem:
            own = eng_sem[eng]
            if not (sem is own and k):
                if sem is None or not k:
                    sem, k = own, 1
                else:
                    pass
            v = own.n
            if v > 0:
                q(eng, lambda e, s=own, v=v: e.wait_ge(s.h, v))
        tgt = None
        incs = []
        if sem is not None and k:
            sem.n += k
            tgt = sem.n
            incs.append((sem, k))
        if _SIM_SAFE and eng in eng_sem and eng_sem[eng] is not sem:
            own = eng_sem[eng]
            own.n += 1
            incs.append((own, 1))
        def fn(e, make=make, incs=tuple(incs)):
            inst = make(e)
            for s, kk in incs:
                inst.then_inc(s.h, kk)
        q(eng, fn)
        return tgt

    def qw(eng, sem, val=None):
        v = sem.n if val is None else val
        if v is None or v <= 0:
            return
        q(eng, lambda e, s=sem, v=v: e.wait_ge(s.h, v))

    def barrier():
        for eng in ENGINES:
            for s in allsems:
                qw(eng, s)

    def dram_in(name, shape, dt=F32):
        return nc.dram_tensor(name, list(shape), dt, kind="ExternalInput").ap()

    d_x = dram_in("x", (T, D))
    d_wqt = dram_in("wqt", (D, D))
    d_wkt = dram_in("wkt", (D, D))
    d_wvt = dram_in("wvt", (D, D))
    d_wot = dram_in("wot", (D, D))
    d_bo = dram_in("bo", (1, D))
    d_lq1 = dram_in("lq1", (1, DH))
    d_lq2 = dram_in("lq2", (1, DH))
    d_lk1 = dram_in("lk1", (1, DH))
    d_lk2 = dram_in("lk2", (1, DH))
    d_wa = dram_in("wa", (D, NEXP))
    d_rg = dram_in("rg", (NEXP, D, FF))
    d_rgb = dram_in("rgb", (NEXP, FF))
    d_rw1 = dram_in("rw1", (NEXP, D, FF))
    d_rb1 = dram_in("rb1", (NEXP, FF))
    d_rw2 = dram_in("rw2", (NEXP, FF, D))
    d_rb2 = dram_in("rb2", (NEXP, D))
    d_sg = dram_in("sg", (NSH, D, FF))
    d_sgb = dram_in("sgb", (NSH, FF))
    d_sw1 = dram_in("sw1", (NSH, D, FF))
    d_sb1 = dram_in("sb1", (NSH, FF))
    d_sw2 = dram_in("sw2", (NSH, FF, D))
    d_sb2 = dram_in("sb2", (NSH, D))
    d_i128b = dram_in("i128b", (128, 128), BF16)
    d_i128f = dram_in("i128f", (128, 128))
    d_i16f = dram_in("i16f", (16, 16))
    d_ind_m = dram_in("ind_m", (128, 1))
    d_ind_1m = dram_in("ind_1m", (128, 1))
    d_out = nc.dram_tensor("out", [T, D], F32, kind="ExternalOutput").ap()
    d_wsc = nc.dram_tensor("wscratch", [4, 16, 128, 8192], BF16, kind="Internal").ap()

    dbg_outs = {}

    def dbg(name, src, waits=()):
        if not _DEBUG:
            return
        ap = nc.dram_tensor("dbg_" + name, list(src.shape), src.dtype, kind="ExternalOutput").ap()
        dbg_outs[name] = ap
        for s, v in waits:
            qw("sync", s, v)
        qi("sync", lambda e, a=ap, sp=src: e.dma_start(out=a, in_=sp), st, 16)
        for engx in ("vector", "scalar", "tensor", "gpsimd", "sync"):
            qw(engx, st)

    ctx = ExitStack()
    sb = lambda name, shape, dt: ctx.enter_context(nc.sbuf_tensor(name, shape, dt))
    ps = lambda name, shape, dt: ctx.enter_context(nc.psum_tensor(name, shape, dt))

    xn = sb("xn", [128, NCH * MS], BF16)
    gbuf = sb("gbuf", [128, (MS // 128) * D], BF16)
    wring = sb("wring", [128, 2 * 8192], BF16)
    fbuf = sb("fbuf", [128, (MS // 128) * D], F32)
    ubuf = sb("ubuf", [128, FC * MS], BF16)
    utmp = sb("utmp", [128, 2 * 512], F32)

    def fb_bf16(c0, n):
        return fbuf[:, c0:c0 + n].bitcast(BF16)

    Q_ST = lambda: fb_bf16(0, 2048)
    K_ST = lambda: fb_bf16(2048, 2048)
    V_SB = lambda: fb_bf16(4096, 2048)
    O_ST = lambda: fb_bf16(6144, 2048)
    VSTA = lambda: fb_bf16(8192, 4096)
    VSTB = lambda: fb_bf16(12288, 4096)

    ubf32 = ubuf[:, :].bitcast(F32)
    e_f32 = ubf32[0:32, 0:512]
    eexp = ubf32[0:32, 512:1024]
    o_sb = ubf32[:, 1024:1536]
    o_sq = ubf32[:, 1536:2048]
    ebf = ubuf[0:32, 6144:6656]
    ebr = ubuf[0:32, 6656:7168]
    at1 = ubuf[0:32, 4096:5120]
    at2 = ubuf[0:32, 5120:6144]

    m_sb = sb("m_sb", [32, 2 * 16], F32)
    z_sb = sb("z_sb", [32, 2 * 16], F32)
    rz_sb = sb("rz_sb", [32, 2 * 16], F32)
    msq_t = utmp[0:1, 0:512]
    rstd1 = utmp[0:1, 512:1024]
    xslice = sb("xslice", [128, 2 * 512], F32)
    at1b = xslice[0:32, 0:512].bitcast(BF16)
    at2b = xslice[0:32, 512:1024].bitcast(BF16)
    outsl = sb("outsl", [128, 2 * 512], F32)
    xtile = fbuf[:, 0:2048]
    xnt = fbuf[:, 2048:3072].bitcast(BF16)
    wa_sb = sb("wa_sb", [128, NCH * NEXP], BF16)
    rgb_sb = fbuf[0:16, 4096:5120]
    rb1_sb = fbuf[0:16, 5120:6144]
    sgb_sb = fbuf[0:2, 6144:7168]
    sb1_sb = fbuf[0:2, 7168:8192]
    rgb_t = sb("rgb_t", [128, 128], F32)
    rb1_t = sb("rb1_t", [128, 128], F32)
    sgb_t = sb("sgb_t", [128, 16], F32)
    sb1_t = sb("sb1_t", [128, 16], F32)
    rb2_sb = sb("rb2_sb", [16, D], BF16)
    sb2_sb = sb("sb2_sb", [2, D], BF16)
    bo_sb = sb("bo_sb", [1, D], BF16)
    lrow = fbuf[0:1, 3072:3072 + 4 * DH]
    lscr = sb("lscr", [1, 8], F32)
    lam_sb = sb("lam_sb", [1, 1], F32)
    rowscale = sb("rowscale", [128, 1], F32)
    ind_m = sb("ind_m_s", [128, 1], F32)
    ind_1m = sb("ind_1m_s", [128, 1], F32)
    i128b = sb("i128b_s", [128, 128], BF16)
    i128f = sb("i128f_s", [128, 128], F32)
    i16f = sb("i16f_s", [16, 16], F32)
    ones1b = sb("ones1b", [1, 128], BF16)
    ones1f = sb("ones1f", [1, 128], F32)
    ones2b = sb("ones2b", [2, 128], BF16)
    onescf = sb("onescf", [128, 1], F32)
    cEPS = sb("cEPS", [128, 1], F32)
    cLN1M = sb("cLN1M", [1, 1], F32)
    aff_sb = sb("aff_sb", [128, 128], F32)
    top_sb = sb("top_sb", [128, 64], F32)
    den_sb = sb("den_sb", [128, 8], F32)
    rden_sb = sb("rden_sb", [128, 8], F32)
    mask_sb = sb("mask_sb", [128, 16], F32)
    cmb_sb = sb("cmb_sb", [128, 16], F32)
    ctm = sb("ctm", [128, 128], F32)
    cfm = sb("cfm", [16, MS], BF16)
    stats = sb("stats", [128, 8], F32)

    pA = ps("pA", [128, 512], F32)
    pB = ps("pB", [128, 512], F32)
    pC = ps("pC", [128, 512], F32)
    pD = ps("pD", [128, 512], F32)
    pS0 = ps("pS0", [128, 512], F32)
    pS1 = ps("pS1", [128, 512], F32)
    pO = ps("pO", [128, 512], F32)
    pX = ps("pX", [128, 512], F32)

    def pXb(i):
        return pX[:, :].bitcast(BF16)[:, i * 128:(i + 1) * 128]

    gates = {}

    def bank_gate(bank, eng="tensor"):
        g = gates.get(id(bank))
        if g:
            qw(eng, g[0], g[1])

    def bank_done(bank, sem):
        gates[id(bank)] = (sem, sem.n)

    ring = {"i": 0, "consumed": {}}
    swr = Sem("swr")
    allsems.append(swr)
    scstate = {}

    def wpiece(src_ap, a_dim, b_dim, cache=None):
        i = ring["i"]
        slot = i % 4
        dst = wring[:, slot * 4096: slot * 4096 + a_dim * b_dim]
        dstv = dst.rearrange("p (a b) -> p a b", a=a_dim)
        prev = ring["consumed"].get(i - 4)
        if prev:
            qw("gpsimd", pe, prev)
        px = ring.get("extra", {}).get(i - 4)
        if px:
            qw("gpsimd", px[0], px[1])
        wsem = [wv0, wv1, wv2, wv3][slot]
        if cache is not None and scstate.get(cache):
            mi, pi = cache
            qw("gpsimd", swr, scstate[cache])
            tgt = qi("gpsimd", lambda e, d=dst, mi=mi, pi=pi: e.dma_start(
                out=d, in_=d_wsc[mi, pi, :, 0:d.shape[1]]), wsem, 16)
        else:
            tgt = qi("gpsimd", lambda e, d=dstv, s=src_ap: e.dma_start(out=d, in_=s), wsem, 16)
            if cache is not None:
                mi, pi = cache
                qw("sync", wsem, tgt)
                qw("sync", swr, swr.n)
                wt = qi("sync", lambda e, d=dst, mi=mi, pi=pi: e.dma_start(
                    out=d_wsc[mi, pi, :, 0:d.shape[1]], in_=d), swr, 16)
                scstate[cache] = wt
                ring.setdefault("extra", {})[i] = (swr, wt)
        ring["i"] = i + 1
        return dst, (wsem, tgt), i

    def wdone(i):
        ring["consumed"][i] = pe.n

    def sload(dst, src):
        return qi("sync", lambda e, d=dst, s=src: e.dma_start(out=d, in_=s), ld, 16)

    for dst, src in [
        (i128b[:, :], d_i128b), (i128f[:, :], d_i128f), (i16f[:, :], d_i16f),
        (ind_m[:, :], d_ind_m), (ind_1m[:, :], d_ind_1m),
        (rgb_sb[:, :], d_rgb), (rb1_sb[:, :], d_rb1),
        (sgb_sb[:, :], d_sgb), (sb1_sb[:, :], d_sb1),
        (lrow[0:1, 0:DH], d_lq1), (lrow[0:1, DH:2 * DH], d_lk1),
        (lrow[0:1, 2 * DH:3 * DH], d_lq2), (lrow[0:1, 3 * DH:4 * DH], d_lk2),
    ]:
        sload(dst, src)
    ld_consts = ld.n

    qi("gpsimd", lambda e: e.dma_start(out=rb2_sb[:, :], in_=d_rb2), wvb, 16)
    qi("gpsimd", lambda e: e.dma_start(out=sb2_sb[:, :], in_=d_sb2), wvb, 16)
    qi("gpsimd", lambda e: e.dma_start(out=bo_sb[:, :], in_=d_bo), wvb, 16)
    qi("gpsimd", lambda e: e.dma_start(
        out=wa_sb[:, :].rearrange("p (c e) -> p c e", c=NCH),
        in_=d_wa.rearrange("(c p) e -> p c e", p=128)), wvb, 16)
    wv_bias = wvb.n
    for buf in (ones1b, ones1f, ones2b, onescf):
        qi("gpsimd", lambda e, b=buf: e.memset(b[:, :], 1.0), gp, 1)
    qi("gpsimd", lambda e: e.memset(cEPS[:, :], EPS), gp, 1)
    qi("gpsimd", lambda e: e.memset(cLN1M[:, :], LN1M), gp, 1)
    qi("gpsimd", lambda e: e.memset(VSTA()[0:32, :], 0.0), gp, 1)
    qi("gpsimd", lambda e: e.memset(VSTB()[0:32, :], 0.0), gp, 1)
    gp_zero = gp.n

    qw("tensor", ld, ld_consts)
    for c in range(8):
        qi("tensor", lambda e, c=c: e.transpose(pX[:, c * 16:c * 16 + 16], rgb_sb[:, c * 128:(c + 1) * 128], i16f[:, :]), pe, 1)
        qi("tensor", lambda e, c=c: e.transpose(pX[:, 128 + c * 16:128 + c * 16 + 16], rb1_sb[:, c * 128:(c + 1) * 128], i16f[:, :]), pe, 1)
        qi("tensor", lambda e, c=c: e.transpose(pX[:, 256 + c * 2:256 + c * 2 + 2], sgb_sb[:, c * 128:(c + 1) * 128], i16f[0:2, 0:2]), pe, 1)
        qi("tensor", lambda e, c=c: e.transpose(pX[:, 272 + c * 2:272 + c * 2 + 2], sb1_sb[:, c * 128:(c + 1) * 128], i16f[0:2, 0:2]), pe, 1)
    qw("vector", pe)
    qi("vector", lambda e: e.tensor_copy(rgb_t[:, :], pX[:, 0:128]), dv, 1)
    qi("vector", lambda e: e.tensor_copy(rb1_t[:, :], pX[:, 128:256]), dv, 1)
    qi("vector", lambda e: e.tensor_copy(sgb_t[:, :], pX[:, 256:272]), dv, 1)
    qi("vector", lambda e: e.tensor_copy(sb1_t[:, :], pX[:, 272:288]), dv, 1)
    dv_bias = dv.n
    bank_done(pX, dv)

    qw("vector", ld, ld_consts)
    qi("vector", lambda e: e.tensor_mul(lrow[0:1, 0:DH], lrow[0:1, 0:DH], lrow[0:1, DH:2 * DH]), dv, 1)
    qi("vector", lambda e: e.reduce_sum(lscr[0:1, 0:1], lrow[0:1, 0:DH], axis=AX.X), dv, 1)
    qi("vector", lambda e: e.tensor_mul(lrow[0:1, 2 * DH:3 * DH], lrow[0:1, 2 * DH:3 * DH], lrow[0:1, 3 * DH:4 * DH]), dv, 1)
    qi("vector", lambda e: e.reduce_sum(lscr[0:1, 1:2], lrow[0:1, 2 * DH:3 * DH], axis=AX.X), dv, 1)
    qw("scalar", dv)
    qi("scalar", lambda e: e.activation(lscr[0:1, 2:3], lscr[0:1, 0:1], AF.Exp), ac, 1)
    qi("scalar", lambda e: e.activation(lscr[0:1, 3:4], lscr[0:1, 1:2], AF.Exp), ac, 1)
    qw("vector", ac)
    qi("vector", lambda e: e.tensor_sub(lscr[0:1, 4:5], lscr[0:1, 2:3], lscr[0:1, 3:4]), dv, 1)
    qi("vector", lambda e: e.tensor_scalar_add(lam_sb[0:1, 0:1], lscr[0:1, 4:5], LAM0), dv, 1)
    dv_lam = dv.n
    bank_gate(pX)
    qw("tensor", dv, dv_lam)
    qw("tensor", gp)
    qi("tensor", lambda e: e.matmul(pX[:, 0:1], ones1f[:, :], lam_sb[0:1, 0:1], start=True, stop=True), pe, 1)
    qw("vector", pe)
    qw("vector", ld, ld_consts)
    qi("vector", lambda e: e.tensor_scalar(rowscale[:, :], pX[:, 0:1], ind_m[:, :], ind_1m[:, :], ALU.mult, ALU.add), dv, 1)
    dv_rowscale = dv.n
    bank_done(pX, dv)

    slot_guard = {0: (0, 0), 1: (0, 0)}
    for ms in range(NMSLAB):
        ms_t0 = ms * MS
        NT = MS // 128
        barrier()

        prev_tile = {"dv_xnt": 0, "ac_sq": 0, "pe_tr": 0}
        for it in range(NT):
            t0 = ms_t0 + it * 128
            qw("sync", dv, prev_tile["dv_xnt"])
            qw("sync", ac, prev_tile["ac_sq"])
            sload(xtile[:, :], d_x[t0:t0 + 128, :])
            ld_x = ld.n
            qw("vector", ld, ld_x)
            qi("vector", lambda e: e.reduce_sum(stats[:, 0:1], xtile[:, :], axis=AX.X), dv, 1)
            qw("scalar", ld, ld_x)
            for s4 in range(4):
                qi("scalar", lambda e, s4=s4: e.activation(
                    o_sb[:, :], xtile[:, s4 * 512:(s4 + 1) * 512], AF.Square,
                    accum_out=stats[:, 1 + s4:2 + s4]), ac, 1)
            ac_sq = ac.n
            qi("vector", lambda e: e.tensor_scalar_mul(stats[:, 5:6], stats[:, 0:1], 1.0 / D), dv, 1)
            qw("vector", ac, ac_sq)
            qi("vector", lambda e: e.tensor_add(stats[:, 1:2], stats[:, 1:2], stats[:, 2:3]), dv, 1)
            qi("vector", lambda e: e.tensor_add(stats[:, 3:4], stats[:, 3:4], stats[:, 4:5]), dv, 1)
            qi("vector", lambda e: e.tensor_add(stats[:, 1:2], stats[:, 1:2], stats[:, 3:4]), dv, 1)
            qi("vector", lambda e: e.tensor_mul(stats[:, 6:7], stats[:, 5:6], stats[:, 5:6]), dv, 1)
            qi("vector", lambda e: e.tensor_scalar_mul(stats[:, 7:8], stats[:, 1:2], 1.0 / D), dv, 1)
            qi("vector", lambda e: e.tensor_sub(stats[:, 7:8], stats[:, 7:8], stats[:, 6:7]), dv, 1)
            dv_var = dv.n
            qw("scalar", dv, dv_var)
            qi("scalar", lambda e: e.activation(stats[:, 1:2], stats[:, 7:8], AF.Ln, bias=cEPS[:, :]), ac, 1)
            qi("scalar", lambda e: e.activation(stats[:, 2:3], stats[:, 1:2], AF.Exp, scale=-0.5), ac, 1)
            ac_rstd = ac.n
            qw("vector", ac, ac_rstd)
            qw("vector", pe, prev_tile["pe_tr"])
            qi("vector", lambda e: e.tensor_scalar(xnt[:, :], xtile[:, :], stats[:, 5:6], stats[:, 2:3], ALU.subtract, ALU.mult), dv, 1)
            dv_xnt = dv.n
            bank_gate(pX)
            qw("tensor", dv, dv_xnt)
            for half in range(2):
                for cc in range(8):
                    c = half * 8 + cc
                    qi("tensor", lambda e, c=c, cc=cc: e.transpose(pXb(cc), xnt[:, c * 128:(c + 1) * 128], i128b[:, :]), pe, 1)
                pe_tr = pe.n
                qw("vector", pe, pe_tr)
                for cc in range(8):
                    c = half * 8 + cc
                    qi("vector", lambda e, c=c, cc=cc, it=it: e.tensor_copy(
                        xn[:, c * MS + it * 128: c * MS + it * 128 + 128], pXb(cc)), dv, 1)
                if half == 0:
                    qw("tensor", dv)
            bank_done(pX, dv)
            prev_tile = {"dv_xnt": dv_xnt, "ac_sq": ac_sq, "pe_tr": pe.n}
        dv_xn = dv.n
        dbg("xn_ms%d" % ms, xn[:, :], waits=[(dv, dv_xn)])

        bank_gate(pC)
        qw("tensor", dv, dv_xn)
        qw("tensor", wvb, wv_bias)
        for i in range(8):
            for kc in range(NCH):
                qi("tensor", lambda e, i=i, kc=kc: e.matmul(
                    pC[:, i * 16:(i + 1) * 16],
                    xn[:, kc * MS + i * 128: kc * MS + (i + 1) * 128],
                    wa_sb[:, kc * 16:(kc + 1) * 16],
                    start=(kc == 0), stop=(kc == NCH - 1)),
                    pe, 1 if kc == NCH - 1 else 0)
        pe_aff = pe.n
        qw("scalar", pe, pe_aff)
        qi("scalar", lambda e: e.activation(aff_sb[:, :], pC[:, 0:128], AF.Sigmoid), ac, 1)
        ac_aff = ac.n
        bank_done(pC, ac)
        qw("vector", ac, ac_aff)
        for i in range(8):
            qi("vector", lambda e, i=i: e.max(top_sb[:, i * 8:(i + 1) * 8], aff_sb[:, i * 16:(i + 1) * 16]), dv, 1)
        for i in range(8):
            qi("vector", lambda e, i=i: e.tensor_add(den_sb[:, i:i + 1], top_sb[:, i * 8:i * 8 + 1], top_sb[:, i * 8 + 1:i * 8 + 2]), dv, 1)
        qi("vector", lambda e: e.reciprocal(rden_sb[:, :], den_sb[:, :]), dv, 1)
        dv_top = dv.n
        qw("vector", dv, dv_top)
        for i in range(8):
            qi("vector", lambda e, i=i: e.tensor_single_scalar(mask_sb[:, :], aff_sb[:, i * 16:(i + 1) * 16], top_sb[:, i * 8 + 1:i * 8 + 2], ALU.is_ge), dv, 1)
            qi("vector", lambda e, i=i: e.tensor_mul(cmb_sb[:, :], aff_sb[:, i * 16:(i + 1) * 16], mask_sb[:, :]), dv, 1)
            qi("vector", lambda e, i=i: e.tensor_scalar_mul(ctm[:, i * 16:(i + 1) * 16], cmb_sb[:, :], rden_sb[:, i:i + 1]), dv, 1)
        dv_ctm = dv.n
        dbg("aff_ms%d" % ms, aff_sb[:, :], waits=[(ac, ac_aff)])
        dbg("top_ms%d" % ms, top_sb[:, :], waits=[(dv, dv_top)])
        dbg("ctm_ms%d" % ms, ctm[:, :], waits=[(dv, dv_ctm)])
        for rnd in range(2):
            bank_gate(pX)
            qw("tensor", dv, dv_ctm)
            for i4 in range(4):
                i = rnd * 4 + i4
                qi("tensor", lambda e, i=i, i4=i4: e.transpose(pX[0:16, i4 * 128:(i4 + 1) * 128], ctm[:, i * 16:(i + 1) * 16], i128f[:, :]), pe, 1)
            pe_cfm = pe.n
            qw("vector", pe, pe_cfm)
            for i4 in range(4):
                i = rnd * 4 + i4
                qi("vector", lambda e, i=i, i4=i4: e.tensor_copy(cfm[0:16, i * 128:(i + 1) * 128], pX[0:16, i4 * 128:(i4 + 1) * 128]), dv, 1)
            bank_done(pX, dv)
        dv_cfm = dv.n

        qw("tensor", dv, dv_cfm)
        qw("tensor", wvb, wv_bias)
        for sub in range(8):
            for dblk in range(4):
                bank = [pC, pD][(sub * 4 + dblk) % 2]
                bank_gate(bank)
                qi("tensor", lambda e, b=bank, sub=sub, dblk=dblk: e.matmul(
                    b[:, :], cfm[0:16, sub * 128:(sub + 1) * 128],
                    rb2_sb[:, dblk * 512:(dblk + 1) * 512], start=True, stop=False), pe, 0)
                qi("tensor", lambda e, b=bank, dblk=dblk: e.matmul(
                    b[:, :], ones2b[:, :], sb2_sb[:, dblk * 512:(dblk + 1) * 512],
                    start=False, stop=False), pe, 0)
                for j in range(4):
                    c = dblk * 4 + j
                    qi("tensor", lambda e, b=bank, c=c, j=j, sub=sub: e.matmul(
                        b[:, j * 128:(j + 1) * 128],
                        xn[:, c * MS + sub * 128: c * MS + (sub + 1) * 128],
                        i128b[:, :], start=False, stop=(j == 3)),
                        pe, 1 if j == 3 else 0)
                pe_fi = pe.n
                qw("vector", pe, pe_fi)
                qi("vector", lambda e, b=bank, sub=sub, dblk=dblk: e.tensor_copy(
                    fbuf[:, sub * 2048 + dblk * 512: sub * 2048 + (dblk + 1) * 512], b[:, :]), dv, 1)
                bank_done(bank, dv)
        qw("vector", dv, dv_ctm)
        qw("vector", dv, dv_bias)
        qw("scalar", dv, dv_bias)

        prev_dv_u = {0: 0, 1: 0}
        for ex in range(NEXP + NSH):
            shared = ex >= NEXP
            jj = ex - NEXP
            if shared:
                dgm, dw1m, dw2m = d_sg[jj], d_sw1[jj], d_sw2[jj]
                gbias = lambda c, jj=jj: sgb_t[:, c * 2 + jj:c * 2 + jj + 1]
                bbias = lambda c, jj=jj: sb1_t[:, c * 2 + jj:c * 2 + jj + 1]
            else:
                dgm, dw1m, dw2m = d_rg[ex], d_rw1[ex], d_rw2[ex]
                gbias = lambda c, ex=ex: rgb_t[:, c * 16 + ex:c * 16 + ex + 1]
                bbias = lambda c, ex=ex: rb1_t[:, c * 16 + ex:c * 16 + ex + 1]

            for fq in range(4):
                gsrc = dgm[:, fq * 256:(fq + 1) * 256].rearrange("(c p) f -> p c f", p=128)
                w1src = dw1m[:, fq * 256:(fq + 1) * 256].rearrange("(c p) f -> p c f", p=128)
                gdst, gw, gi_ = wpiece(gsrc, NCH, 256)
                w1dst, w1w, w1i = wpiece(w1src, NCH, 256)
                for fcl in range(2):
                    fc = fq * 2 + fcl
                    if fcl % 2 == 0:
                        Ab, Bb = [pA, pB], [pC, pD]
                    else:
                        Ab, Bb = [pS0, pS1], [pO, pX]
                    qw("tensor", gw[0], gw[1])
                    for ss in range(2):
                        bank_gate(Ab[ss])
                        for kc in range(NCH):
                            qi("tensor", lambda e, b=Ab[ss], g=gdst, kc=kc, fcl=fcl, ss=ss: e.matmul(
                                b[:, :],
                                g.rearrange("p (c f) -> p c f", c=NCH)[:, kc, fcl * 128:(fcl + 1) * 128],
                                xn[:, kc * MS + ss * 512: kc * MS + ss * 512 + 512],
                                start=(kc == 0), stop=(kc == NCH - 1)),
                                pe, 1 if kc == NCH - 1 else 0)
                    qw("tensor", w1w[0], w1w[1])
                    for ss in range(2):
                        bank_gate(Bb[ss])
                        for kc in range(NCH):
                            qi("tensor", lambda e, b=Bb[ss], g=w1dst, kc=kc, fcl=fcl, ss=ss: e.matmul(
                                b[:, :],
                                g.rearrange("p (c f) -> p c f", c=NCH)[:, kc, fcl * 128:(fcl + 1) * 128],
                                xn[:, kc * MS + ss * 512: kc * MS + ss * 512 + 512],
                                start=(kc == 0), stop=(kc == NCH - 1)),
                                pe, 1 if kc == NCH - 1 else 0)
                    pe_ab = pe.n
                    for ss in range(2):
                        qw("scalar", pe, pe_ab)
                        qw("scalar", dv, prev_dv_u[ss])
                        qi("scalar", lambda e, b=Ab[ss], ss=ss, fc=fc: e.activation(
                            utmp[:, ss * 512:(ss + 1) * 512], b[:, :], AF.Gelu, bias=gbias(fc)), ac, 1)
                    ac_g = ac.n
                    qw("vector", ac, ac_g)
                    for ss in range(2):
                        tgt = qi("vector", lambda e, bB=Bb[ss], ss=ss, fc=fc: e.scalar_tensor_tensor(
                            ubuf[:, fc * MS + ss * 512: fc * MS + ss * 512 + 512],
                            bB[:, :], bbias(fc), utmp[:, ss * 512:(ss + 1) * 512],
                            ALU.add, ALU.mult), dv, 1)
                        prev_dv_u[ss] = tgt
                    for ss in range(2):
                        bank_done(Ab[ss], ac)
                        bank_done(Bb[ss], dv)
                wdone(gi_)
                wdone(w1i)

            dv_u_all = dv.n
            for dq in range(4):
                w2src = dw2m[:, dq * 512:(dq + 1) * 512].rearrange("(c p) d -> p c d", p=128)
                w2dst, w2w, w2i = wpiece(w2src, FC, 512)
                qw("tensor", w2w[0], w2w[1])
                qw("tensor", dv, dv_u_all)
                for sub in range(8):
                    dblk = dq
                    bank = [pA, pC, pS0, pO][sub % 4]
                    bank_gate(bank)
                    for kc in range(FC):
                        qi("tensor", lambda e, b=bank, kc=kc, sub=sub, w2=w2dst: e.matmul(
                            b[:, :],
                            ubuf[:, kc * MS + sub * 128: kc * MS + (sub + 1) * 128],
                            w2.rearrange("p (c d) -> p c d", c=FC)[:, kc, :],
                            start=(kc == 0), stop=(kc == FC - 1)),
                            pe, 1 if kc == FC - 1 else 0)
                    pe_w2 = pe.n
                    qw("vector", pe, pe_w2)
                    if shared:
                        qi("vector", lambda e, b=bank, sub=sub, dblk=dblk: e.tensor_add(
                            fbuf[:, sub * 2048 + dblk * 512: sub * 2048 + (dblk + 1) * 512],
                            b[:, :],
                            fbuf[:, sub * 2048 + dblk * 512: sub * 2048 + (dblk + 1) * 512]), dv, 1)
                    else:
                        qi("vector", lambda e, b=bank, sub=sub, dblk=dblk, ex=ex: e.scalar_tensor_tensor(
                            fbuf[:, sub * 2048 + dblk * 512: sub * 2048 + (dblk + 1) * 512],
                            b[:, :], ctm[:, sub * 16 + ex:sub * 16 + ex + 1],
                            fbuf[:, sub * 2048 + dblk * 512: sub * 2048 + (dblk + 1) * 512],
                            ALU.mult, ALU.add), dv, 1)
                    bank_done(bank, dv)
                wdone(w2i)
        dv_f = dv.n
        dbg("f_ms%d" % ms, fbuf[:, 0:2048], waits=[(dv, dv_f)])

        qw("scalar", dv, dv_f)
        ac_silu = []
        for it in range(NT):
            ac_silu.append(qi("scalar", lambda e, it=it: e.activation(
                gbuf[:, it * 2048:(it + 1) * 2048], fbuf[:, it * 2048:(it + 1) * 2048], AF.Silu), ac, 1))
        ac_g_all = ac.n
        dbg("g_ms%d" % ms, gbuf[:, 0:2048], waits=[(ac, ac_g_all)])

        qw("gpsimd", ac, ac_silu[7])
        qw("gpsimd", pe)
        qw("gpsimd", dv)
        qi("gpsimd", lambda e: e.memset(VSTA()[0:32, :], 0.0), gp, 1)
        qi("gpsimd", lambda e: e.memset(VSTB()[0:32, :], 0.0), gp, 1)
        gp_zero = gp.n

        for sl in range(NSLAB):
            st0 = ms_t0 + sl * SLAB
            ls0 = sl * SLAB
            qst, kst, vsb, ost = Q_ST(), K_ST(), V_SB(), O_ST()
            vsts = [VSTA(), VSTB()]

            for mi, (dmat, kind) in enumerate([(d_wqt, "q"), (d_wkt, "k"), (d_wvt, "v")]):
                for ei in range(8):
                    src = dmat[:, ei * 256:(ei + 1) * 256].rearrange("(c p) f -> p c f", p=128)
                    wdst, ww, wi = wpiece(src, NCH, 256, cache=(mi, ei))
                    qw("tensor", ww[0], ww[1])
                    for jjq in range(2):
                        ochunk = ei * 2 + jjq
                        bank = [pA, pB, pC, pD][ochunk % 4]
                        bank_gate(bank)
                        for kc in range(NCH):
                            qi("tensor", lambda e, b=bank, w=wdst, kc=kc, jjq=jjq, ls0=ls0: e.matmul(
                                b[:, 0:SLAB],
                                w.rearrange("p (c f) -> p c f", c=NCH)[:, kc, jjq * 128:(jjq + 1) * 128],
                                xn[:, kc * MS + ls0: kc * MS + ls0 + SLAB],
                                start=(kc == 0), stop=(kc == NCH - 1)),
                                pe, 1 if kc == NCH - 1 else 0)
                        pe_pr = pe.n
                        qw("vector", pe, pe_pr)
                        qw("vector", ac, ac_silu[2] if kind == "v" else (ac_silu[0] if kind == "q" else ac_silu[1]))
                        if kind == "v":
                            qi("vector", lambda e, b=bank, oc=ochunk: e.tensor_copy(
                                vsb[:, oc * SLAB:(oc + 1) * SLAB], b[:, 0:SLAB]), dv, 1)
                        else:
                            dst = qst if kind == "q" else kst
                            scale = 0.125 if kind == "q" else 1.0
                            for eps_ in range(2):
                                a_idx = 2 * ochunk + eps_
                                for r in range(2):
                                    qi("vector", lambda e, b=bank, dstv=dst, eps_=eps_, r=r, a=a_idx, sc=scale: e.tensor_scalar_mul(
                                        dstv[r * 64:(r + 1) * 64, a * 128:(a + 1) * 128],
                                        b[eps_ * 64:(eps_ + 1) * 64, r * 128:(r + 1) * 128], sc), dv, 1)
                        bank_done(bank, dv)
                    wdone(wi)
            dv_proj = dv.n
            if sl == 0:
                dbg("qst_ms%d" % ms, qst[:, :], waits=[(dv, dv_proj)])
                dbg("kst_ms%d" % ms, kst[:, :], waits=[(dv, dv_proj)])
                dbg("vsb_ms%d" % ms, vsb[:, :], waits=[(dv, dv_proj)])

            vd_t = {}

            def v_scatter(g, buf, bi_, pe_guard):
                qw("sync", dv, dv_proj)
                qw("sync", gp, gp_zero)
                if pe_guard:
                    qw("sync", pe, pe_guard)
                vs = vda[bi_]
                for h in range(16):
                    qi("sync", lambda e, h=h, g=g, b=buf: e.dma_start(
                        out=b[h:h + 1, :].rearrange("r (ee t) -> r ee t", ee=128),
                        in_=vsb[:, h * SLAB + g * 64: h * SLAB + (g + 1) * 64]), vs, 16)
                vd_t[g] = (vs, vs.n)

            v_scatter(0, vsts[0], 0, 0)
            v_scatter(2, vsts[1], 1, 0)

            if _DEBUG and ms == 0 and sl == 0:
                qw("sync", vda[0], vda[0].n)
                qw("sync", vda[1], vda[1].n)
                dbg("vsta_s0", VSTA()[0:32, :], waits=[])
                dbg("vstb_s0", VSTB()[0:32, :], waits=[])
            sbgrp = [[pS0, pS1], [pB, pC]]
            opair = [pO, pA]
            obk = [pX, pD]
            at_set = [(at1, at2), (at1b, at2b)]
            pe_s_rec = {}
            dv_at = {}
            pe_diffv = {-1: 0, -2: 0}
            st_o = {}

            def emit_S(p):
                for r in range(2):
                    sbank = sbgrp[p % 2][r]
                    bank_gate(sbank)
                    qw("tensor", dv, dv_proj)
                    for sl_tok in range(16):
                        ur = p * 16 + sl_tok
                        qi("tensor", lambda e, sb_=sbank, r=r, ur=ur, sl_tok=sl_tok: e.matmul(
                            sb_[0:32, sl_tok * 32:(sl_tok + 1) * 32],
                            qst[r * 64:(r + 1) * 64, :].rearrange("p (a t) -> p t a", t=128)[:, ur, :],
                            kst[r * 64:(r + 1) * 64, :].rearrange("p (a t) -> p t a", t=128)[:, ur, :],
                            start=True, stop=True),
                            pe, 1 if sl_tok == 15 else 0)
                pe_s_rec[p] = pe.n

            emit_S(0)
            for p in range(9):
                if p < 8:
                    a1, a2 = at_set[p % 2]
                    for r in range(2):
                        sbank = sbgrp[p % 2][r]
                        qw("vector", pe, pe_s_rec[p])
                        qw("vector", pe, pe_diffv[p - 2])
                        qi("vector", lambda e, sb_=sbank, r=r: e.reduce_max(
                            m_sb[:, r * 16:(r + 1) * 16],
                            sb_[0:32, :].rearrange("p (t b) -> p t b", b=32), axis=AX.X), dv, 1)
                        qi("vector", lambda e, sb_=sbank, r=r: e.tensor_tensor(
                            eexp[:, :].rearrange("p (t b) -> p t b", b=32),
                            sb_[0:32, :].rearrange("p (t b) -> p t b", b=32),
                            m_sb[:, r * 16:(r + 1) * 16].unsqueeze(2).to_broadcast((32, 16, 32)),
                            ALU.subtract), dv, 1)
                        dv_e = dv.n
                        bank_done(sbank, dv)
                        qw("scalar", dv, dv_e)
                        qi("scalar", lambda e: e.activation(eexp[:, :], eexp[:, :], AF.Exp), ac, 1)
                        ac_e = ac.n
                        qw("vector", ac, ac_e)
                        qi("vector", lambda e, r=r: e.reduce_sum(
                            z_sb[:, r * 16:(r + 1) * 16],
                            eexp[:, :].rearrange("p (t b) -> p t b", b=32), axis=AX.X), dv, 1)
                        qi("vector", lambda e, r=r: e.reciprocal(rz_sb[:, r * 16:(r + 1) * 16], z_sb[:, r * 16:(r + 1) * 16]), dv, 1)
                        qw("vector", dv, dv_rowscale)
                        qi("vector", lambda e, r=r: e.scalar_tensor_tensor(
                            ebf[0:32, :].rearrange("p (t b) -> p t b", b=32),
                            eexp[:, :].rearrange("p (t b) -> p t b", b=32),
                            rowscale[0:32, :],
                            rz_sb[:, r * 16:(r + 1) * 16].unsqueeze(2).to_broadcast((32, 16, 32)),
                            ALU.mult, ALU.mult), dv, 1)
                        qi("vector", lambda e, a1=a1, r=r: e.transpose(a1[:, r * 512:(r + 1) * 512], ebf[0:32, :]), dv, 1)
                        qi("vector", lambda e: e.tensor_copy(
                            ebr[:, :].rearrange("p (t h i) -> p t h i", t=16, h=2),
                            bass.AP(ubuf, 6144 + 16, [[8192, 32], [32, 16], [-16, 2], [1, 16]])), dv, 1)
                        qi("vector", lambda e, a2=a2, r=r: e.transpose(
                            a2[:, r * 512:(r + 1) * 512], ebr[:, :]), dv, 1)
                        dv_at[(p, r)] = dv.n
                    emit_S(p + 1) if p + 1 < 8 else None
                    if p == 4:
                        v_scatter(1, vsts[0], 0, pe_diffv[3])
                        v_scatter(3, vsts[1], 1, pe_diffv[3])
                if p >= 1:
                    pm = p - 1
                    bcb = obk[pm % 2]
                    qw("vector", pe, st_o[pm]["pe_bc"])
                    qw("vector", ac, ac_silu[3])
                    qi("vector", lambda e, pm=pm, ostv=ost, bcb=bcb: e.tensor_mul(
                        ostv.rearrange("q (a r i) -> q a r i", a=16, r=2)[:, :, :, pm * 16:(pm + 1) * 16],
                        o_sb[:, :].rearrange("q (r i a) -> q a r i", r=2, i=16),
                        bcb[:, :].rearrange("q (r i a) -> q a r i", r=2, i=16)), dv, 1)
                    bank_done(bcb, dv)
                if p < 8:
                    obank = opair[p % 2]
                    bank_gate(obank)
                    for r in range(2):
                        qw("tensor", dv, dv_at[(p, r)])
                        gq = (0 if p < 4 else 1) if r == 0 else (2 if p < 4 else 3)
                        vbuf = vsts[r]
                        qw("tensor", vd_t[gq][0], vd_t[gq][1])
                        a1, a2 = at_set[p % 2]
                        for sl_tok in range(16):
                            u = r * 128 + p * 16 + sl_tok
                            ugl = u % 64
                            pairtok = r * 16 + sl_tok
                            qi("tensor", lambda e, ob=obank, vb=vbuf, ugl=ugl, a1=a1, r=r, sl_tok=sl_tok, pairtok=pairtok: e.matmul(
                                ob[:, pairtok * 16:(pairtok + 1) * 16],
                                vb[0:32, :].rearrange("p (ee t) -> p t ee", t=64)[:, ugl, :],
                                a1[:, r * 512 + sl_tok * 32: r * 512 + sl_tok * 32 + 16],
                                start=True, stop=False), pe, 0)
                            qi("tensor", lambda e, ob=obank, vb=vbuf, ugl=ugl, a2=a2, r=r, sl_tok=sl_tok, pairtok=pairtok: e.matmul(
                                ob[:, pairtok * 16:(pairtok + 1) * 16],
                                vb[0:32, :].rearrange("p (ee t) -> p t ee", t=64)[:, ugl, :],
                                a2[:, r * 512 + sl_tok * 32 + 16: r * 512 + sl_tok * 32 + 32],
                                start=False, stop=True),
                                pe, 1 if sl_tok == 15 else 0)
                    pe_diffv[p] = pe.n
                    qw("vector", pe, pe_diffv[p])
                    qi("vector", lambda e, ob=obank: e.tensor_copy(o_sb[:, :], ob[:, :]), dv, 1)
                    dv_ocp = dv.n
                    if _DEBUG and ms == 0 and sl == 0 and p == 0:
                        dbg("at1_p0", at1[:, :], waits=[(dv, dv_at[(0, 1)])])
                        dbg("at2_p0", at2[:, :], waits=[(dv, dv_at[(0, 1)])])
                        dbg("osb_p0", o_sb[:, :], waits=[(dv, dv_ocp)])
                    bank_done(obank, dv)
                    qw("scalar", dv, dv_ocp)
                    if p >= 1:
                        qw("scalar", pe, st_o[p - 1]["pe_msq"])
                    qi("scalar", lambda e: e.activation(o_sq[:, :], o_sb[:, :], AF.Square), ac, 1)
                    ac_osq = ac.n
                    mb = obk[p % 2]
                    bank_gate(mb)
                    qw("tensor", ac, ac_osq)
                    qi("tensor", lambda e, mb=mb: e.matmul(mb[0:1, :], onescf[:, :], o_sq[:, :], start=True, stop=True), pe, 1)
                    pe_msq = pe.n
                    qw("scalar", pe, pe_msq)
                    qi("scalar", lambda e, mb=mb: e.activation(msq_t[0:1, :], mb[0:1, :], AF.Ln, bias=cEPS[0:1, :], scale=1.0 / EH), ac, 1)
                    qi("scalar", lambda e: e.activation(rstd1[0:1, :], msq_t[0:1, :], AF.Exp, scale=-0.5, bias=cLN1M[0:1, :]), ac, 1)
                    ac_rst = ac.n
                    qw("tensor", ac, ac_rst)
                    qi("tensor", lambda e, mb=mb: e.matmul(mb[:, :], ones1f[:, :], rstd1[0:1, :], start=True, stop=True), pe, 1)
                    st_o[p] = {"pe_msq": pe_msq, "pe_bc": pe.n}
            dv_att_done = dv.n
            if sl == 0:
                dbg("ost_ms%d" % ms, ost[:, :], waits=[(dv, dv_att_done)])

            qw("sync", pe)
            for dq in range(4):
                src0 = d_wot[:, dq * 512:(dq + 1) * 512].rearrange("(a p) d -> p a d", p=128)
                wdst0, ww0, wi0 = wpiece(src0[:, 0:8, :], 8, 512, cache=(3, dq * 2))
                wdst1, ww1, wi1 = wpiece(src0[:, 8:16, :], 8, 512, cache=(3, dq * 2 + 1))
                qw("tensor", ww0[0], ww0[1])
                qw("tensor", ww1[0], ww1[1])
                for tsub in range(2):
                    t0g = st0 + tsub * 128
                    slot = tsub
                    qw("sync", dv, slot_guard[slot][0])
                    qi("sync", lambda e, t0g=t0g, dq=dq, slot=slot: e.dma_start(
                        out=xslice[:, slot * 512:(slot + 1) * 512],
                        in_=d_x[t0g:t0g + 128, dq * 512:(dq + 1) * 512]), lds[slot], 16)
                    ld_xs = lds[slot].n
                    bank = [pB, pC][tsub]
                    bank_gate(bank)
                    qw("tensor", dv, dv_att_done)
                    for a in range(16):
                        w_ = wdst0 if a < 8 else wdst1
                        qi("tensor", lambda e, b=bank, a=a, tsub=tsub, w=w_, ostv=ost: e.matmul(
                            b[:, :],
                            ostv[:, a * SLAB + tsub * 128: a * SLAB + tsub * 128 + 128],
                            w.rearrange("p (a d) -> p a d", a=8)[:, a % 8, :],
                            start=(a == 0), stop=False), pe, 0)
                    qi("tensor", lambda e, b=bank, dq=dq: e.matmul(
                        b[:, :], ones1b[:, :], bo_sb[0:1, dq * 512:(dq + 1) * 512],
                        start=False, stop=True), pe, 1)
                    pe_wo = pe.n
                    qw("vector", pe, pe_wo)
                    qw("vector", lds[slot], ld_xs)
                    qw("vector", sts[slot], slot_guard[slot][1])
                    gi = sl * 2 + tsub
                    qw("vector", ac, ac_silu[gi])
                    qi("vector", lambda e, b=bank, gi=gi, dq=dq, slot=slot: e.tensor_mul(
                        outsl[:, slot * 512:(slot + 1) * 512],
                        b[:, :], gbuf[:, gi * 2048 + dq * 512: gi * 2048 + (dq + 1) * 512]), dv, 1)
                    dv_out = qi("vector", lambda e, slot=slot: e.tensor_add(
                        outsl[:, slot * 512:(slot + 1) * 512],
                        outsl[:, slot * 512:(slot + 1) * 512],
                        xslice[:, slot * 512:(slot + 1) * 512]), dv, 1)
                    bank_done(bank, dv)
                    qw("sync", dv, dv_out)
                    st_tgt = qi("sync", lambda e, t0g=t0g, dq=dq, slot=slot: e.dma_start(
                        out=d_out[t0g:t0g + 128, dq * 512:(dq + 1) * 512],
                        in_=outsl[:, slot * 512:(slot + 1) * 512]), sts[slot], 16)
                    slot_guard[slot] = (dv_out, st_tgt)
                wdone(wi0)
                wdone(wi1)

    qw("sync", st)
    qw("sync", sts[0])
    qw("sync", sts[1])
    qw("sync", vda[0])
    qw("sync", vda[1])

    semctx = ExitStack()
    for s in allsems:
        s.h = semctx.enter_context(nc.semaphore("sem_" + s.name))
    with nc.Block() as blk:
        blk.sync(lambda e: [f(e) for f in Q["sync"]])
        blk.gpsimd(lambda e: [f(e) for f in Q["gpsimd"]])
        blk.vector(lambda e: [f(e) for f in Q["vector"]])
        blk.scalar(lambda e: [f(e) for f in Q["scalar"]])
        blk.tensor(lambda e: [f(e) for f in Q["tensor"]])
    semctx.close()
    ctx.close()
    return nc, dbg_outs


def _host_inputs(inputs):
    f32 = np.float32
    g = lambda k: np.asarray(inputs[k], dtype=f32)
    perm = np.zeros(D, dtype=np.int64)
    for a in range(32):
        c, h = divmod(a, 16)
        for d in range(DH):
            perm[a * DH + d] = h * 128 + 2 * d + c
    shared = {
        "wqt": np.ascontiguousarray(g("L0_attn_Wq")[perm, :].T),
        "wkt": np.ascontiguousarray(g("L0_attn_Wk")[perm, :].T),
        "wvt": np.ascontiguousarray(g("L0_attn_Wv").T),
        "wot": np.ascontiguousarray(g("L0_attn_Wo").T),
        "bo": np.ascontiguousarray(g("L0_attn_bo").reshape(1, D)),
        "lq1": np.ascontiguousarray(g("L0_attn_lq1").reshape(1, DH)),
        "lq2": np.ascontiguousarray(g("L0_attn_lq2").reshape(1, DH)),
        "lk1": np.ascontiguousarray(g("L0_attn_lk1").reshape(1, DH)),
        "lk2": np.ascontiguousarray(g("L0_attn_lk2").reshape(1, DH)),
        "wa": np.ascontiguousarray(g("L0_ffn_wa")),
        "rg": np.ascontiguousarray(g("L0_ffn_rg")),
        "rgb": np.ascontiguousarray(g("L0_ffn_rgb")),
        "rw1": np.ascontiguousarray(g("L0_ffn_rw1")),
        "rb1": np.ascontiguousarray(g("L0_ffn_rb1")),
        "rw2": np.ascontiguousarray(g("L0_ffn_rw2")),
        "rb2": np.ascontiguousarray(g("L0_ffn_rb2")),
        "sg": np.ascontiguousarray(g("L0_ffn_sg")),
        "sgb": np.ascontiguousarray(g("L0_ffn_sgb")),
        "sw1": np.ascontiguousarray(g("L0_ffn_sw1")),
        "sb1": np.ascontiguousarray(g("L0_ffn_sb1")),
        "sw2": np.ascontiguousarray(g("L0_ffn_sw2")),
        "sb2": np.ascontiguousarray(g("L0_ffn_sb2")),
        "i128b": np.eye(128).astype(ml_dtypes.bfloat16),
        "i128f": np.eye(128, dtype=f32),
        "i16f": np.eye(16, dtype=f32),
        "ind_m": (-((np.arange(128) % 32) >= 16).astype(f32)).reshape(128, 1),
        "ind_1m": (1.0 - ((np.arange(128) % 32) >= 16).astype(f32)).reshape(128, 1),
    }
    return shared


_LAST_RESULTS = None


def kernel(**inputs):
    global _LAST_RESULTS
    x = np.asarray(inputs["L0_attn_x"], dtype=np.float32).reshape(16384, D)
    shared = _host_inputs(inputs)
    nc, dbg_outs = build_program()
    in_maps = []
    for c in range(NCORES):
        m = dict(shared)
        m["x"] = np.ascontiguousarray(x[c * T:(c + 1) * T, :])
        in_maps.append(m)
    res = run_bass_kernel_spmd(nc, in_maps, core_ids=list(range(NCORES)))
    _LAST_RESULTS = res.results
    out = np.concatenate([res.results[c]["out"] for c in range(NCORES)], axis=0)
    return out.reshape(4, 4096, D)


# version 07
# speedup vs baseline: 1.3776x; 1.0699x over previous
from contextlib import ExitStack

import numpy as np
import ml_dtypes

import concourse.bass as bass
import concourse.mybir as mybir
from concourse.bass_utils import run_bass_kernel_spmd

F32 = mybir.dt.float32
BF16 = mybir.dt.bfloat16
AF = mybir.ActivationFunctionType
ALU = mybir.AluOpType
AX = mybir.AxisListType

_DEBUG = False
_SIM_SAFE = True
_SIM_COMPAT = False

NCORES = 8
T = 2048
MS = 1024
NMSLAB = T // MS
SLAB = 256
NSLAB = MS // SLAB
D = 2048
NCH = D // 128
FF = 1024
FC = FF // 128
NEXP = 16
NSH = 2
H = 16
DH = 64
EH = 128
LAM0 = 0.8 - 0.6 * float(np.exp(-0.3 * 12))
LN1M = float(np.log(1.0 - LAM0))
EPS = 1e-5

ENGINES = ("sync", "gpsimd", "vector", "scalar", "tensor")


class Sem:
    def __init__(self, name):
        self.name = name
        self.n = 0
        self.h = None


def build_program():
    nc = bass.Bass("TRN2", target_bir_lowering=False)
    Q = {e: [] for e in ENGINES}
    ld, wv, vd, st, pe, dv, ac, gp = (Sem(s) for s in ("ld", "wv", "vd", "st", "pe", "dv", "ac", "gp"))
    wvb = Sem("wvb"); wv0 = Sem("wv0"); wv1 = Sem("wv1"); wv2 = Sem("wv2"); wv3 = Sem("wv3")
    lds = [Sem("lds0"), Sem("lds1")]
    sts = [Sem("sts0"), Sem("sts1")]
    vda = [Sem("vda"), Sem("vdb")]
    allsems = [ld, wv, vd, st, pe, dv, ac, gp, wvb, wv0, wv1, wv2, wv3] + lds + sts + vda

    def q(eng, fn):
        Q[eng].append(fn)

    eng_sem = {"vector": dv, "scalar": ac}

    def qi(eng, make, sem=None, k=1):
        if _SIM_SAFE and eng in eng_sem:
            own = eng_sem[eng]
            if not (sem is own and k):
                if sem is None or not k:
                    sem, k = own, 1
                else:
                    pass
            v = own.n
            if v > 0:
                q(eng, lambda e, s=own, v=v: e.wait_ge(s.h, v))
        tgt = None
        incs = []
        if sem is not None and k:
            sem.n += k
            tgt = sem.n
            incs.append((sem, k))
        if _SIM_SAFE and eng in eng_sem and eng_sem[eng] is not sem:
            own = eng_sem[eng]
            own.n += 1
            incs.append((own, 1))
        def fn(e, make=make, incs=tuple(incs)):
            inst = make(e)
            for s, kk in incs:
                inst.then_inc(s.h, kk)
        q(eng, fn)
        return tgt

    def qw(eng, sem, val=None):
        v = sem.n if val is None else val
        if v is None or v <= 0:
            return
        q(eng, lambda e, s=sem, v=v: e.wait_ge(s.h, v))

    def barrier():
        for eng in ENGINES:
            for s in allsems:
                qw(eng, s)

    def dram_in(name, shape, dt=F32):
        return nc.dram_tensor(name, list(shape), dt, kind="ExternalInput").ap()

    d_x = dram_in("x", (T, D))
    d_wqt = dram_in("wqt", (D, D))
    d_wkt = dram_in("wkt", (D, D))
    d_wvt = dram_in("wvt", (D, D))
    d_wot = dram_in("wot", (D, D))
    d_bo = dram_in("bo", (1, D))
    d_lq1 = dram_in("lq1", (1, DH))
    d_lq2 = dram_in("lq2", (1, DH))
    d_lk1 = dram_in("lk1", (1, DH))
    d_lk2 = dram_in("lk2", (1, DH))
    d_wa = dram_in("wa", (D, NEXP))
    d_rg = dram_in("rg", (NEXP, D, FF))
    d_rgb = dram_in("rgb", (NEXP, FF))
    d_rw1 = dram_in("rw1", (NEXP, D, FF))
    d_rb1 = dram_in("rb1", (NEXP, FF))
    d_rw2 = dram_in("rw2", (NEXP, FF, D))
    d_rb2 = dram_in("rb2", (NEXP, D))
    d_sg = dram_in("sg", (NSH, D, FF))
    d_sgb = dram_in("sgb", (NSH, FF))
    d_sw1 = dram_in("sw1", (NSH, D, FF))
    d_sb1 = dram_in("sb1", (NSH, FF))
    d_sw2 = dram_in("sw2", (NSH, FF, D))
    d_sb2 = dram_in("sb2", (NSH, D))
    d_i128b = dram_in("i128b", (128, 128), BF16)
    d_i128f = dram_in("i128f", (128, 128))
    d_i16f = dram_in("i16f", (16, 16))
    d_ind_m = dram_in("ind_m", (128, 1))
    d_ind_1m = dram_in("ind_1m", (128, 1))
    d_out = nc.dram_tensor("out", [T, D], F32, kind="ExternalOutput").ap()
    d_wsc = nc.dram_tensor("wscratch", [4, 16, 128, 8192], BF16, kind="Internal").ap()

    dbg_outs = {}

    def dbg(name, src, waits=()):
        if not _DEBUG:
            return
        ap = nc.dram_tensor("dbg_" + name, list(src.shape), src.dtype, kind="ExternalOutput").ap()
        dbg_outs[name] = ap
        for s, v in waits:
            qw("sync", s, v)
        qi("sync", lambda e, a=ap, sp=src: e.dma_start(out=a, in_=sp), st, 16)
        for engx in ("vector", "scalar", "tensor", "gpsimd", "sync"):
            qw(engx, st)

    ctx = ExitStack()
    sb = lambda name, shape, dt: ctx.enter_context(nc.sbuf_tensor(name, shape, dt))
    ps = lambda name, shape, dt: ctx.enter_context(nc.psum_tensor(name, shape, dt))

    xn = sb("xn", [128, NCH * MS], BF16)
    gbuf = sb("gbuf", [128, (MS // 128) * D], BF16)
    wring = sb("wring", [128, 2 * 8192], BF16)
    fbuf = sb("fbuf", [128, (MS // 128) * D], F32)
    ubuf = sb("ubuf", [128, FC * MS], BF16)
    utmp = sb("utmp", [128, 2 * 512], F32)

    def fb_bf16(c0, n):
        return fbuf[:, c0:c0 + n].bitcast(BF16)

    Q_ST = lambda: fb_bf16(0, 2048)
    K_ST = lambda: fb_bf16(2048, 2048)
    V_SB = lambda: fb_bf16(4096, 2048)
    O_ST = lambda: fb_bf16(6144, 2048)
    VSTA = lambda: fb_bf16(8192, 4096)
    VSTB = lambda: fb_bf16(12288, 4096)

    ubf32 = ubuf[:, :].bitcast(F32)
    e_f32 = ubf32[0:32, 0:512]
    eexp = ubf32[0:32, 512:1024]
    o_sb = ubf32[:, 1024:1536]
    o_sq = ubuf[:, 3072:3584]
    ebf = ubuf[0:32, 6144:6656]
    ebr = ubuf[0:32, 6656:7168]
    at1 = ubuf[0:32, 4096:5120]
    at2 = ubuf[0:32, 5120:6144]

    m_sb = sb("m_sb", [32, 2 * 16], F32)
    z_sb = sb("z_sb", [32, 2 * 16], F32)
    rz_sb = sb("rz_sb", [32, 2 * 16], F32)
    msq_t = utmp[0:1, 0:512]
    rstd1 = utmp[0:1, 512:1024]
    xslice = sb("xslice", [128, 2 * 512], F32)
    at1b = xslice[0:32, 0:512].bitcast(BF16)
    at2b = xslice[0:32, 512:1024].bitcast(BF16)
    outsl = sb("outsl", [128, 2 * 512], F32)
    xtiles = [fbuf[:, 0:2048], fbuf[:, 8192:10240]]
    xnts = [fbuf[:, 2048:3072].bitcast(BF16), fbuf[:, 10240:11264].bitcast(BF16)]
    wa_sb = sb("wa_sb", [128, NCH * NEXP], BF16)
    rgb_sb = fbuf[0:16, 4096:5120]
    rb1_sb = fbuf[0:16, 5120:6144]
    sgb_sb = fbuf[0:2, 6144:7168]
    sb1_sb = fbuf[0:2, 7168:8192]
    rgb_t = sb("rgb_t", [128, 128], F32)
    rb1_t = sb("rb1_t", [128, 128], F32)
    sgb_t = sb("sgb_t", [128, 16], F32)
    sb1_t = sb("sb1_t", [128, 16], F32)
    rb2_sb = sb("rb2_sb", [16, D], BF16)
    sb2_sb = sb("sb2_sb", [2, D], BF16)
    bo_sb = sb("bo_sb", [1, D], BF16)
    lrow = fbuf[0:1, 3072:3072 + 4 * DH]
    lscr = sb("lscr", [1, 8], F32)
    lam_sb = sb("lam_sb", [1, 1], F32)
    rowscale = sb("rowscale", [128, 1], F32)
    ind_m = sb("ind_m_s", [128, 1], F32)
    ind_1m = sb("ind_1m_s", [128, 1], F32)
    i128b = sb("i128b_s", [128, 128], BF16)
    i128f = sb("i128f_s", [128, 128], F32)
    i16f = sb("i16f_s", [16, 16], F32)
    ones1b = sb("ones1b", [1, 128], BF16)
    ones1f = sb("ones1f", [1, 128], F32)
    ones2b = sb("ones2b", [2, 128], BF16)
    onescf = sb("onescf", [128, 1], F32)
    onescb = sb("onescb", [128, 1], BF16)
    cEPS = sb("cEPS", [128, 1], F32)
    cLN1M = sb("cLN1M", [1, 1], F32)
    aff_sb = sb("aff_sb", [128, 128], F32)
    top_sb = sb("top_sb", [128, 64], F32)
    den_sb = sb("den_sb", [128, 8], F32)
    rden_sb = sb("rden_sb", [128, 8], F32)
    mask_sb = sb("mask_sb", [128, 16], F32)
    cmb_sb = sb("cmb_sb", [128, 16], F32)
    ctm = sb("ctm", [128, 128], F32)
    cfm = sb("cfm", [16, MS], BF16)
    stats = sb("stats", [128, 16], F32)

    pA = ps("pA", [128, 512], F32)
    pB = ps("pB", [128, 512], F32)
    pC = ps("pC", [128, 512], F32)
    pD = ps("pD", [128, 512], F32)
    pS0 = ps("pS0", [128, 512], F32)
    pS1 = ps("pS1", [128, 512], F32)
    pO = ps("pO", [128, 512], F32)
    pX = ps("pX", [128, 512], F32)

    def pXb(i):
        return pX[:, :].bitcast(BF16)[:, i * 128:(i + 1) * 128]

    gates = {}

    def bank_gate(bank, eng="tensor"):
        g = gates.get(id(bank))
        if g:
            qw(eng, g[0], g[1])

    def bank_done(bank, sem):
        gates[id(bank)] = (sem, sem.n)

    ring = {"i": 0, "consumed": {}}
    swr = Sem("swr")
    allsems.append(swr)
    scstate = {}

    def wpiece(src_ap, a_dim, b_dim, cache=None):
        i = ring["i"]
        slot = i % 4
        dst = wring[:, slot * 4096: slot * 4096 + a_dim * b_dim]
        dstv = dst.rearrange("p (a b) -> p a b", a=a_dim)
        prev = ring["consumed"].get(i - 4)
        if prev:
            qw("gpsimd", pe, prev)
        px = ring.get("extra", {}).get(i - 4)
        if px:
            qw("gpsimd", px[0], px[1])
        wsem = [wv0, wv1, wv2, wv3][slot]
        if cache is not None and scstate.get(cache):
            mi, pi = cache
            qw("gpsimd", swr, scstate[cache])
            tgt = qi("gpsimd", lambda e, d=dst, mi=mi, pi=pi: e.dma_start(
                out=d, in_=d_wsc[mi, pi, :, 0:d.shape[1]]), wsem, 16)
        else:
            tgt = qi("gpsimd", lambda e, d=dstv, s=src_ap: e.dma_start(out=d, in_=s), wsem, 16)
            if cache is not None:
                mi, pi = cache
                qw("sync", wsem, tgt)
                qw("sync", swr, swr.n)
                wt = qi("sync", lambda e, d=dst, mi=mi, pi=pi: e.dma_start(
                    out=d_wsc[mi, pi, :, 0:d.shape[1]], in_=d), swr, 16)
                scstate[cache] = wt
                ring.setdefault("extra", {})[i] = (swr, wt)
        ring["i"] = i + 1
        return dst, (wsem, tgt), i

    def wdone(i):
        ring["consumed"][i] = pe.n

    def sload(dst, src):
        return qi("sync", lambda e, d=dst, s=src: e.dma_start(out=d, in_=s), ld, 16)

    for dst, src in [
        (i128b[:, :], d_i128b), (i128f[:, :], d_i128f), (i16f[:, :], d_i16f),
        (ind_m[:, :], d_ind_m), (ind_1m[:, :], d_ind_1m),
        (rgb_sb[:, :], d_rgb), (rb1_sb[:, :], d_rb1),
        (sgb_sb[:, :], d_sgb), (sb1_sb[:, :], d_sb1),
        (lrow[0:1, 0:DH], d_lq1), (lrow[0:1, DH:2 * DH], d_lk1),
        (lrow[0:1, 2 * DH:3 * DH], d_lq2), (lrow[0:1, 3 * DH:4 * DH], d_lk2),
    ]:
        sload(dst, src)
    ld_consts = ld.n

    qi("gpsimd", lambda e: e.dma_start(out=rb2_sb[:, :], in_=d_rb2), wvb, 16)
    qi("gpsimd", lambda e: e.dma_start(out=sb2_sb[:, :], in_=d_sb2), wvb, 16)
    qi("gpsimd", lambda e: e.dma_start(out=bo_sb[:, :], in_=d_bo), wvb, 16)
    qi("gpsimd", lambda e: e.dma_start(
        out=wa_sb[:, :].rearrange("p (c e) -> p c e", c=NCH),
        in_=d_wa.rearrange("(c p) e -> p c e", p=128)), wvb, 16)
    wv_bias = wvb.n
    for buf in (ones1b, ones1f, ones2b, onescf, onescb):
        qi("gpsimd", lambda e, b=buf: e.memset(b[:, :], 1.0), gp, 1)
    qi("gpsimd", lambda e: e.memset(cEPS[:, :], EPS), gp, 1)
    qi("gpsimd", lambda e: e.memset(cLN1M[:, :], LN1M), gp, 1)
    qi("gpsimd", lambda e: e.memset(VSTA()[0:32, :], 0.0), gp, 1)
    qi("gpsimd", lambda e: e.memset(VSTB()[0:32, :], 0.0), gp, 1)
    gp_zero = gp.n

    qw("tensor", ld, ld_consts)
    for c in range(8):
        qi("tensor", lambda e, c=c: e.transpose(pX[:, c * 16:c * 16 + 16], rgb_sb[:, c * 128:(c + 1) * 128], i16f[:, :]), pe, 1)
        qi("tensor", lambda e, c=c: e.transpose(pX[:, 128 + c * 16:128 + c * 16 + 16], rb1_sb[:, c * 128:(c + 1) * 128], i16f[:, :]), pe, 1)
        qi("tensor", lambda e, c=c: e.transpose(pX[:, 256 + c * 2:256 + c * 2 + 2], sgb_sb[:, c * 128:(c + 1) * 128], i16f[0:2, 0:2]), pe, 1)
        qi("tensor", lambda e, c=c: e.transpose(pX[:, 272 + c * 2:272 + c * 2 + 2], sb1_sb[:, c * 128:(c + 1) * 128], i16f[0:2, 0:2]), pe, 1)
    qw("vector", pe)
    qi("vector", lambda e: e.tensor_copy(rgb_t[:, :], pX[:, 0:128]), dv, 1)
    qi("vector", lambda e: e.tensor_copy(rb1_t[:, :], pX[:, 128:256]), dv, 1)
    qi("vector", lambda e: e.tensor_copy(sgb_t[:, :], pX[:, 256:272]), dv, 1)
    qi("vector", lambda e: e.tensor_copy(sb1_t[:, :], pX[:, 272:288]), dv, 1)
    dv_bias = dv.n
    bank_done(pX, dv)

    qw("vector", ld, ld_consts)
    qi("vector", lambda e: e.tensor_mul(lrow[0:1, 0:DH], lrow[0:1, 0:DH], lrow[0:1, DH:2 * DH]), dv, 1)
    qi("vector", lambda e: e.reduce_sum(lscr[0:1, 0:1], lrow[0:1, 0:DH], axis=AX.X), dv, 1)
    qi("vector", lambda e: e.tensor_mul(lrow[0:1, 2 * DH:3 * DH], lrow[0:1, 2 * DH:3 * DH], lrow[0:1, 3 * DH:4 * DH]), dv, 1)
    qi("vector", lambda e: e.reduce_sum(lscr[0:1, 1:2], lrow[0:1, 2 * DH:3 * DH], axis=AX.X), dv, 1)
    qw("scalar", dv)
    qi("scalar", lambda e: e.activation(lscr[0:1, 2:3], lscr[0:1, 0:1], AF.Exp), ac, 1)
    qi("scalar", lambda e: e.activation(lscr[0:1, 3:4], lscr[0:1, 1:2], AF.Exp), ac, 1)
    qw("vector", ac)
    qi("vector", lambda e: e.tensor_sub(lscr[0:1, 4:5], lscr[0:1, 2:3], lscr[0:1, 3:4]), dv, 1)
    qi("vector", lambda e: e.tensor_scalar_add(lam_sb[0:1, 0:1], lscr[0:1, 4:5], LAM0), dv, 1)
    dv_lam = dv.n
    bank_gate(pX)
    qw("tensor", dv, dv_lam)
    qw("tensor", gp)
    qi("tensor", lambda e: e.matmul(pX[:, 0:1], ones1f[:, :], lam_sb[0:1, 0:1], start=True, stop=True), pe, 1)
    qw("vector", pe)
    qw("vector", ld, ld_consts)
    qi("vector", lambda e: e.tensor_scalar(rowscale[:, :], pX[:, 0:1], ind_m[:, :], ind_1m[:, :], ALU.mult, ALU.add), dv, 1)
    dv_rowscale = dv.n
    bank_done(pX, dv)

    slot_guard = {0: (0, 0), 1: (0, 0)}
    for ms in range(NMSLAB):
        ms_t0 = ms * MS
        NT = MS // 128
        barrier()

        prev_tile = [{"dv_xnt": 0, "ac_sq": 0}, {"dv_xnt": 0, "ac_sq": 0}]
        prev_pe_tr = [0, 0]
        stats_guard = [0, 0]
        for it in range(NT):
            t0 = ms_t0 + it * 128
            xtile = xtiles[it % 2]
            xnt = xnts[it % 2]
            qw("sync", dv, prev_tile[it % 2]["dv_xnt"])
            qw("sync", ac, prev_tile[it % 2]["ac_sq"])
            qi("sync", lambda e, xtile=xtile, t0=t0: e.dma_start(out=xtile[:, :], in_=d_x[t0:t0 + 128, :]), lds[it % 2], 16)
            ld_x = lds[it % 2].n
            so = (it % 2) * 8
            qw("vector", lds[it % 2], ld_x)
            qi("vector", lambda e, so=so, xtile=xtile: e.reduce_sum(stats[:, so + 0:so + 1], xtile[:, :], axis=AX.X), dv, 1)
            qw("scalar", lds[it % 2], ld_x)
            for s4 in range(4):
                qi("scalar", lambda e, s4=s4, so=so, xtile=xtile: e.activation(
                    o_sb[:, :], xtile[:, s4 * 512:(s4 + 1) * 512], AF.Square,
                    accum_out=stats[:, so + 1 + s4:so + 2 + s4]), ac, 1)
            ac_sq = ac.n
            qi("vector", lambda e, so=so, xtile=xtiles[it % 2], xnt=xnts[it % 2]: e.tensor_scalar_mul(stats[:, so + 5:so + 6], stats[:, so + 0:so + 1], 1.0 / D), dv, 1)
            qw("vector", ac, ac_sq)
            qi("vector", lambda e, so=so, xtile=xtiles[it % 2], xnt=xnts[it % 2]: e.tensor_add(stats[:, so + 1:so + 2], stats[:, so + 1:so + 2], stats[:, so + 2:so + 3]), dv, 1)
            qi("vector", lambda e, so=so, xtile=xtiles[it % 2], xnt=xnts[it % 2]: e.tensor_add(stats[:, so + 3:so + 4], stats[:, so + 3:so + 4], stats[:, so + 4:so + 5]), dv, 1)
            qi("vector", lambda e, so=so, xtile=xtiles[it % 2], xnt=xnts[it % 2]: e.tensor_add(stats[:, so + 1:so + 2], stats[:, so + 1:so + 2], stats[:, so + 3:so + 4]), dv, 1)
            qi("vector", lambda e, so=so, xtile=xtiles[it % 2], xnt=xnts[it % 2]: e.tensor_mul(stats[:, so + 6:so + 7], stats[:, so + 5:so + 6], stats[:, so + 5:so + 6]), dv, 1)
            qi("vector", lambda e, so=so, xtile=xtiles[it % 2], xnt=xnts[it % 2]: e.tensor_scalar_mul(stats[:, so + 7:so + 8], stats[:, so + 1:so + 2], 1.0 / D), dv, 1)
            qi("vector", lambda e, so=so, xtile=xtiles[it % 2], xnt=xnts[it % 2]: e.tensor_sub(stats[:, so + 7:so + 8], stats[:, so + 7:so + 8], stats[:, so + 6:so + 7]), dv, 1)
            dv_var = dv.n
            qw("scalar", dv, dv_var)
            qi("scalar", lambda e, so=so, xtile=xtiles[it % 2], xnt=xnts[it % 2]: e.activation(stats[:, so + 1:so + 2], stats[:, so + 7:so + 8], AF.Ln, bias=cEPS[:, :]), ac, 1)
            qi("scalar", lambda e, so=so, xtile=xtiles[it % 2], xnt=xnts[it % 2]: e.activation(stats[:, so + 2:so + 3], stats[:, so + 1:so + 2], AF.Exp, scale=-0.5), ac, 1)
            ac_rstd = ac.n
            qw("vector", ac, ac_rstd)
            qw("vector", pe, prev_pe_tr[it % 2])
            qi("vector", lambda e, so=so, xtile=xtiles[it % 2], xnt=xnts[it % 2]: e.tensor_scalar(xnt[:, :], xtile[:, :], stats[:, so + 5:so + 6], stats[:, so + 2:so + 3], ALU.subtract, ALU.mult), dv, 1)
            dv_xnt = dv.n
            bank_gate(pX)
            qw("tensor", dv, dv_xnt)
            for half in range(2):
                for cc in range(8):
                    c = half * 8 + cc
                    qi("tensor", lambda e, c=c, cc=cc, xnt=xnt: e.transpose(pXb(cc), xnt[:, c * 128:(c + 1) * 128], i128b[:, :]), pe, 1)
                pe_tr = pe.n
                qw("vector", pe, pe_tr)
                for cc in range(8):
                    c = half * 8 + cc
                    qi("vector", lambda e, c=c, cc=cc, it=it: e.tensor_copy(
                        xn[:, c * MS + it * 128: c * MS + it * 128 + 128], pXb(cc)), dv, 1)
                if half == 0:
                    qw("tensor", dv)
            bank_done(pX, dv)
            prev_tile[it % 2] = {"dv_xnt": dv_xnt, "ac_sq": ac_sq}
            prev_pe_tr[it % 2] = pe.n
        dv_xn = dv.n
        dbg("xn_ms%d" % ms, xn[:, :], waits=[(dv, dv_xn)])

        bank_gate(pC)
        qw("tensor", dv, dv_xn)
        qw("tensor", wvb, wv_bias)
        for i in range(8):
            for kc in range(NCH):
                qi("tensor", lambda e, i=i, kc=kc: e.matmul(
                    pC[:, i * 16:(i + 1) * 16],
                    xn[:, kc * MS + i * 128: kc * MS + (i + 1) * 128],
                    wa_sb[:, kc * 16:(kc + 1) * 16],
                    start=(kc == 0), stop=(kc == NCH - 1)),
                    pe, 1 if kc == NCH - 1 else 0)
        pe_aff = pe.n
        qw("scalar", pe, pe_aff)
        qi("scalar", lambda e: e.activation(aff_sb[:, :], pC[:, 0:128], AF.Sigmoid), ac, 1)
        ac_aff = ac.n
        bank_done(pC, ac)
        qw("vector", ac, ac_aff)
        for i in range(8):
            qi("vector", lambda e, i=i: e.max(top_sb[:, i * 8:(i + 1) * 8], aff_sb[:, i * 16:(i + 1) * 16]), dv, 1)
        for i in range(8):
            qi("vector", lambda e, i=i: e.tensor_add(den_sb[:, i:i + 1], top_sb[:, i * 8:i * 8 + 1], top_sb[:, i * 8 + 1:i * 8 + 2]), dv, 1)
        qi("vector", lambda e: e.reciprocal(rden_sb[:, :], den_sb[:, :]), dv, 1)
        dv_top = dv.n
        qw("vector", dv, dv_top)
        for i in range(8):
            qi("vector", lambda e, i=i: e.tensor_single_scalar(mask_sb[:, :], aff_sb[:, i * 16:(i + 1) * 16], top_sb[:, i * 8 + 1:i * 8 + 2], ALU.is_ge), dv, 1)
            qi("vector", lambda e, i=i: e.tensor_mul(cmb_sb[:, :], aff_sb[:, i * 16:(i + 1) * 16], mask_sb[:, :]), dv, 1)
            qi("vector", lambda e, i=i: e.tensor_scalar_mul(ctm[:, i * 16:(i + 1) * 16], cmb_sb[:, :], rden_sb[:, i:i + 1]), dv, 1)
        dv_ctm = dv.n
        dbg("aff_ms%d" % ms, aff_sb[:, :], waits=[(ac, ac_aff)])
        dbg("top_ms%d" % ms, top_sb[:, :], waits=[(dv, dv_top)])
        dbg("ctm_ms%d" % ms, ctm[:, :], waits=[(dv, dv_ctm)])
        for rnd in range(2):
            bank_gate(pX)
            qw("tensor", dv, dv_ctm)
            for i4 in range(4):
                i = rnd * 4 + i4
                qi("tensor", lambda e, i=i, i4=i4: e.transpose(pX[0:16, i4 * 128:(i4 + 1) * 128], ctm[:, i * 16:(i + 1) * 16], i128f[:, :]), pe, 1)
            pe_cfm = pe.n
            qw("vector", pe, pe_cfm)
            for i4 in range(4):
                i = rnd * 4 + i4
                qi("vector", lambda e, i=i, i4=i4: e.tensor_copy(cfm[0:16, i * 128:(i + 1) * 128], pX[0:16, i4 * 128:(i4 + 1) * 128]), dv, 1)
            bank_done(pX, dv)
        dv_cfm = dv.n

        qw("tensor", dv, dv_cfm)
        qw("tensor", wvb, wv_bias)
        for sub in range(8):
            for dblk in range(4):
                bank = [pC, pD][(sub * 4 + dblk) % 2]
                bank_gate(bank)
                qi("tensor", lambda e, b=bank, sub=sub, dblk=dblk: e.matmul(
                    b[:, :], cfm[0:16, sub * 128:(sub + 1) * 128],
                    rb2_sb[:, dblk * 512:(dblk + 1) * 512], start=True, stop=False), pe, 0)
                qi("tensor", lambda e, b=bank, dblk=dblk: e.matmul(
                    b[:, :], ones2b[:, :], sb2_sb[:, dblk * 512:(dblk + 1) * 512],
                    start=False, stop=False), pe, 0)
                for j in range(4):
                    c = dblk * 4 + j
                    qi("tensor", lambda e, b=bank, c=c, j=j, sub=sub: e.matmul(
                        b[:, j * 128:(j + 1) * 128],
                        xn[:, c * MS + sub * 128: c * MS + (sub + 1) * 128],
                        i128b[:, :], start=False, stop=(j == 3)),
                        pe, 1 if j == 3 else 0)
                pe_fi = pe.n
                qw("vector", pe, pe_fi)
                qi("vector", lambda e, b=bank, sub=sub, dblk=dblk: e.tensor_copy(
                    fbuf[:, sub * 2048 + dblk * 512: sub * 2048 + (dblk + 1) * 512], b[:, :]), dv, 1)
                bank_done(bank, dv)
        qw("vector", dv, dv_ctm)
        qw("vector", dv, dv_bias)
        qw("scalar", dv, dv_bias)

        prev_dv_u = {0: 0, 1: 0}
        for ex in range(NEXP + NSH):
            shared = ex >= NEXP
            jj = ex - NEXP
            if shared:
                dgm, dw1m, dw2m = d_sg[jj], d_sw1[jj], d_sw2[jj]
                gbias = lambda c, jj=jj: sgb_t[:, c * 2 + jj:c * 2 + jj + 1]
                bbias = lambda c, jj=jj: sb1_t[:, c * 2 + jj:c * 2 + jj + 1]
            else:
                dgm, dw1m, dw2m = d_rg[ex], d_rw1[ex], d_rw2[ex]
                gbias = lambda c, ex=ex: rgb_t[:, c * 16 + ex:c * 16 + ex + 1]
                bbias = lambda c, ex=ex: rb1_t[:, c * 16 + ex:c * 16 + ex + 1]

            for fq in range(4):
                gsrc = dgm[:, fq * 256:(fq + 1) * 256].rearrange("(c p) f -> p c f", p=128)
                w1src = dw1m[:, fq * 256:(fq + 1) * 256].rearrange("(c p) f -> p c f", p=128)
                gdst, gw, gi_ = wpiece(gsrc, NCH, 256)
                w1dst, w1w, w1i = wpiece(w1src, NCH, 256)
                for fcl in range(2):
                    fc = fq * 2 + fcl
                    if fcl % 2 == 0:
                        Ab, Bb = [pA, pB], [pC, pD]
                    else:
                        Ab, Bb = [pS0, pS1], [pO, pX]
                    qw("tensor", gw[0], gw[1])
                    for ss in range(2):
                        bank_gate(Ab[ss])
                        for kc in range(NCH):
                            qi("tensor", lambda e, b=Ab[ss], g=gdst, kc=kc, fcl=fcl, ss=ss: e.matmul(
                                b[:, :],
                                g.rearrange("p (c f) -> p c f", c=NCH)[:, kc, fcl * 128:(fcl + 1) * 128],
                                xn[:, kc * MS + ss * 512: kc * MS + ss * 512 + 512],
                                start=(kc == 0), stop=(kc == NCH - 1)),
                                pe, 1 if kc == NCH - 1 else 0)
                    qw("tensor", w1w[0], w1w[1])
                    for ss in range(2):
                        bank_gate(Bb[ss])
                        for kc in range(NCH):
                            qi("tensor", lambda e, b=Bb[ss], g=w1dst, kc=kc, fcl=fcl, ss=ss: e.matmul(
                                b[:, :],
                                g.rearrange("p (c f) -> p c f", c=NCH)[:, kc, fcl * 128:(fcl + 1) * 128],
                                xn[:, kc * MS + ss * 512: kc * MS + ss * 512 + 512],
                                start=(kc == 0), stop=(kc == NCH - 1)),
                                pe, 1 if kc == NCH - 1 else 0)
                    pe_ab = pe.n
                    for ss in range(2):
                        qw("scalar", pe, pe_ab)
                        qw("scalar", dv, prev_dv_u[ss])
                        qi("scalar", lambda e, b=Ab[ss], ss=ss, fc=fc: e.activation(
                            utmp[:, ss * 512:(ss + 1) * 512], b[:, :], AF.Gelu, bias=gbias(fc)), ac, 1)
                    ac_g = ac.n
                    qw("vector", ac, ac_g)
                    for ss in range(2):
                        tgt = qi("vector", lambda e, bB=Bb[ss], ss=ss, fc=fc: e.scalar_tensor_tensor(
                            ubuf[:, fc * MS + ss * 512: fc * MS + ss * 512 + 512],
                            bB[:, :], bbias(fc), utmp[:, ss * 512:(ss + 1) * 512],
                            ALU.add, ALU.mult), dv, 1)
                        prev_dv_u[ss] = tgt
                    for ss in range(2):
                        bank_done(Ab[ss], ac)
                        bank_done(Bb[ss], dv)
                wdone(gi_)
                wdone(w1i)

            dv_u_all = dv.n
            for dq in range(4):
                w2src = dw2m[:, dq * 512:(dq + 1) * 512].rearrange("(c p) d -> p c d", p=128)
                w2dst, w2w, w2i = wpiece(w2src, FC, 512)
                qw("tensor", w2w[0], w2w[1])
                qw("tensor", dv, dv_u_all)
                for sub in range(8):
                    dblk = dq
                    bank = [pA, pC, pS0, pO][sub % 4]
                    bank_gate(bank)
                    for kc in range(FC):
                        qi("tensor", lambda e, b=bank, kc=kc, sub=sub, w2=w2dst: e.matmul(
                            b[:, :],
                            ubuf[:, kc * MS + sub * 128: kc * MS + (sub + 1) * 128],
                            w2.rearrange("p (c d) -> p c d", c=FC)[:, kc, :],
                            start=(kc == 0), stop=(kc == FC - 1)),
                            pe, 1 if kc == FC - 1 else 0)
                    pe_w2 = pe.n
                    qw("vector", pe, pe_w2)
                    if shared:
                        qi("vector", lambda e, b=bank, sub=sub, dblk=dblk: e.tensor_add(
                            fbuf[:, sub * 2048 + dblk * 512: sub * 2048 + (dblk + 1) * 512],
                            b[:, :],
                            fbuf[:, sub * 2048 + dblk * 512: sub * 2048 + (dblk + 1) * 512]), dv, 1)
                    else:
                        qi("vector", lambda e, b=bank, sub=sub, dblk=dblk, ex=ex: e.scalar_tensor_tensor(
                            fbuf[:, sub * 2048 + dblk * 512: sub * 2048 + (dblk + 1) * 512],
                            b[:, :], ctm[:, sub * 16 + ex:sub * 16 + ex + 1],
                            fbuf[:, sub * 2048 + dblk * 512: sub * 2048 + (dblk + 1) * 512],
                            ALU.mult, ALU.add), dv, 1)
                    bank_done(bank, dv)
                wdone(w2i)
        dv_f = dv.n
        dbg("f_ms%d" % ms, fbuf[:, 0:2048], waits=[(dv, dv_f)])

        qw("scalar", dv, dv_f)
        ac_silu = []
        for it in range(NT):
            ac_silu.append(qi("scalar", lambda e, it=it: e.activation(
                gbuf[:, it * 2048:(it + 1) * 2048], fbuf[:, it * 2048:(it + 1) * 2048], AF.Silu), ac, 1))
        ac_g_all = ac.n
        dbg("g_ms%d" % ms, gbuf[:, 0:2048], waits=[(ac, ac_g_all)])

        qw("gpsimd", ac, ac_silu[7])
        qw("gpsimd", pe)
        qw("gpsimd", dv)
        qi("gpsimd", lambda e: e.memset(VSTA()[0:32, :], 0.0), gp, 1)
        qi("gpsimd", lambda e: e.memset(VSTB()[0:32, :], 0.0), gp, 1)
        gp_zero = gp.n

        for sl in range(NSLAB):
            st0 = ms_t0 + sl * SLAB
            ls0 = sl * SLAB
            qst, kst, vsb, ost = Q_ST(), K_ST(), V_SB(), O_ST()
            vsts = [VSTA(), VSTB()]

            for mi, (dmat, kind) in enumerate([(d_wqt, "q"), (d_wkt, "k"), (d_wvt, "v")]):
                for ei in range(8):
                    src = dmat[:, ei * 256:(ei + 1) * 256].rearrange("(c p) f -> p c f", p=128)
                    wdst, ww, wi = wpiece(src, NCH, 256, cache=(mi, ei))
                    qw("tensor", ww[0], ww[1])
                    for jjq in range(2):
                        ochunk = ei * 2 + jjq
                        bank = [pA, pB, pC, pD][ochunk % 4]
                        bank_gate(bank)
                        for kc in range(NCH):
                            qi("tensor", lambda e, b=bank, w=wdst, kc=kc, jjq=jjq, ls0=ls0: e.matmul(
                                b[:, 0:SLAB],
                                w.rearrange("p (c f) -> p c f", c=NCH)[:, kc, jjq * 128:(jjq + 1) * 128],
                                xn[:, kc * MS + ls0: kc * MS + ls0 + SLAB],
                                start=(kc == 0), stop=(kc == NCH - 1)),
                                pe, 1 if kc == NCH - 1 else 0)
                        pe_pr = pe.n
                        qw("vector", pe, pe_pr)
                        qw("vector", ac, ac_silu[2] if kind == "v" else (ac_silu[0] if kind == "q" else ac_silu[1]))
                        if kind == "v":
                            qi("vector", lambda e, b=bank, oc=ochunk: e.tensor_copy(
                                vsb[:, oc * SLAB:(oc + 1) * SLAB], b[:, 0:SLAB]), dv, 1)
                        else:
                            dst = qst if kind == "q" else kst
                            scale = 0.125 if kind == "q" else 1.0
                            for eps_ in range(2):
                                a_idx = 2 * ochunk + eps_
                                for r in range(2):
                                    qi("vector", lambda e, b=bank, dstv=dst, eps_=eps_, r=r, a=a_idx, sc=scale: e.tensor_scalar_mul(
                                        dstv[r * 64:(r + 1) * 64, a * 128:(a + 1) * 128],
                                        b[eps_ * 64:(eps_ + 1) * 64, r * 128:(r + 1) * 128], sc), dv, 1)
                        bank_done(bank, dv)
                    wdone(wi)
            dv_proj = dv.n
            if sl == 0:
                dbg("qst_ms%d" % ms, qst[:, :], waits=[(dv, dv_proj)])
                dbg("kst_ms%d" % ms, kst[:, :], waits=[(dv, dv_proj)])
                dbg("vsb_ms%d" % ms, vsb[:, :], waits=[(dv, dv_proj)])

            vd_t = {}

            def v_scatter(g, buf, bi_, pe_guard):
                qw("sync", dv, dv_proj)
                qw("sync", gp, gp_zero)
                if pe_guard:
                    qw("sync", pe, pe_guard)
                vs = vda[bi_]
                for h in range(16):
                    qi("sync", lambda e, h=h, g=g, b=buf: e.dma_start(
                        out=b[h:h + 1, :].rearrange("r (ee t) -> r ee t", ee=128),
                        in_=vsb[:, h * SLAB + g * 64: h * SLAB + (g + 1) * 64]), vs, 16)
                vd_t[g] = (vs, vs.n)

            v_scatter(0, vsts[0], 0, 0)
            v_scatter(2, vsts[1], 1, 0)

            if _DEBUG and ms == 0 and sl == 0:
                qw("sync", vda[0], vda[0].n)
                qw("sync", vda[1], vda[1].n)
                dbg("vsta_s0", VSTA()[0:32, :], waits=[])
                dbg("vstb_s0", VSTB()[0:32, :], waits=[])
            sbgrp = [[pS0, pS1], [pB, pC]]
            opair = [pO, pA]
            obk = [pX, pD]
            at_set = [(at1, at2), (at1b, at2b)]
            pe_s_rec = {}
            dv_at = {}
            pe_diffv = {-1: 0, -2: 0}
            st_o = {}

            def emit_S(p):
                for r in range(2):
                    sbank = sbgrp[p % 2][r]
                    bank_gate(sbank)
                    qw("tensor", dv, dv_proj)
                    for sl_tok in range(16):
                        ur = p * 16 + sl_tok
                        qi("tensor", lambda e, sb_=sbank, r=r, ur=ur, sl_tok=sl_tok: e.matmul(
                            sb_[0:32, sl_tok * 32:(sl_tok + 1) * 32],
                            qst[r * 64:(r + 1) * 64, :].rearrange("p (a t) -> p t a", t=128)[:, ur, :],
                            kst[r * 64:(r + 1) * 64, :].rearrange("p (a t) -> p t a", t=128)[:, ur, :],
                            start=True, stop=True),
                            pe, 1 if sl_tok == 15 else 0)
                pe_s_rec[p] = pe.n

            emit_S(0)
            for p in range(9):
                if p < 8:
                    a1, a2 = at_set[p % 2]
                    for r in range(2):
                        sbank = sbgrp[p % 2][r]
                        qw("vector", pe, pe_s_rec[p])
                        qw("vector", pe, pe_diffv[p - 2])
                        qi("vector", lambda e, sb_=sbank, r=r: e.reduce_max(
                            m_sb[:, r * 16:(r + 1) * 16],
                            sb_[0:32, :].rearrange("p (t b) -> p t b", b=32), axis=AX.X), dv, 1)
                        qi("vector", lambda e, sb_=sbank, r=r: e.tensor_tensor(
                            eexp[:, :].rearrange("p (t b) -> p t b", b=32),
                            sb_[0:32, :].rearrange("p (t b) -> p t b", b=32),
                            m_sb[:, r * 16:(r + 1) * 16].unsqueeze(2).to_broadcast((32, 16, 32)),
                            ALU.subtract), dv, 1)
                        dv_e = dv.n
                        bank_done(sbank, dv)
                        qw("scalar", dv, dv_e)
                        qi("scalar", lambda e: e.activation(eexp[:, :], eexp[:, :], AF.Exp), ac, 1)
                        ac_e = ac.n
                        qw("vector", ac, ac_e)
                        qi("vector", lambda e, r=r: e.reduce_sum(
                            z_sb[:, r * 16:(r + 1) * 16],
                            eexp[:, :].rearrange("p (t b) -> p t b", b=32), axis=AX.X), dv, 1)
                        qi("vector", lambda e, r=r: e.reciprocal(rz_sb[:, r * 16:(r + 1) * 16], z_sb[:, r * 16:(r + 1) * 16]), dv, 1)
                        qw("vector", dv, dv_rowscale)
                        qi("vector", lambda e, r=r: e.scalar_tensor_tensor(
                            ebf[0:32, :].rearrange("p (t b) -> p t b", b=32),
                            eexp[:, :].rearrange("p (t b) -> p t b", b=32),
                            rowscale[0:32, :],
                            rz_sb[:, r * 16:(r + 1) * 16].unsqueeze(2).to_broadcast((32, 16, 32)),
                            ALU.mult, ALU.mult), dv, 1)
                        qi("vector", lambda e, a1=a1, r=r: e.transpose(a1[:, r * 512:(r + 1) * 512], ebf[0:32, :]), dv, 1)
                        if _SIM_COMPAT:
                            qi("vector", lambda e: e.tensor_copy(
                                ebr[:, :].rearrange("p (t h i) -> p t h i", t=16, h=2),
                                bass.AP(ubuf, 6144 + 16, [[8192, 32], [32, 16], [-16, 2], [1, 16]])), dv, 1)
                            qi("vector", lambda e, a2=a2, r=r: e.transpose(
                                a2[:, r * 512:(r + 1) * 512], ebr[:, :]), dv, 1)
                        else:
                            qi("vector", lambda e, a2=a2, r=r: e.transpose(
                                a2[:, r * 512:(r + 1) * 512],
                                bass.AP(ubuf, 6144 + 16, [[8192, 32], [32, 16], [-16, 2], [1, 16]])), dv, 1)
                        dv_at[(p, r)] = dv.n
                    emit_S(p + 1) if p + 1 < 8 else None
                    if p == 4:
                        v_scatter(1, vsts[0], 0, pe_diffv[3])
                        v_scatter(3, vsts[1], 1, pe_diffv[3])
                if p >= 1:
                    pm = p - 1
                    bcb = obk[pm % 2]
                    qw("vector", pe, st_o[pm]["pe_bc"])
                    qw("vector", ac, ac_silu[3])
                    qi("vector", lambda e, pm=pm, ostv=ost, bcb=bcb: e.tensor_mul(
                        ostv.rearrange("q (a r i) -> q a r i", a=16, r=2)[:, :, :, pm * 16:(pm + 1) * 16],
                        o_sb[:, :].rearrange("q (r i a) -> q a r i", r=2, i=16),
                        bcb[:, :].rearrange("q (r i a) -> q a r i", r=2, i=16)), dv, 1)
                    bank_done(bcb, dv)
                if p < 8:
                    obank = opair[p % 2]
                    bank_gate(obank)
                    for r in range(2):
                        qw("tensor", dv, dv_at[(p, r)])
                        gq = (0 if p < 4 else 1) if r == 0 else (2 if p < 4 else 3)
                        vbuf = vsts[r]
                        qw("tensor", vd_t[gq][0], vd_t[gq][1])
                        a1, a2 = at_set[p % 2]
                        for sl_tok in range(16):
                            u = r * 128 + p * 16 + sl_tok
                            ugl = u % 64
                            pairtok = r * 16 + sl_tok
                            qi("tensor", lambda e, ob=obank, vb=vbuf, ugl=ugl, a1=a1, r=r, sl_tok=sl_tok, pairtok=pairtok: e.matmul(
                                ob[:, pairtok * 16:(pairtok + 1) * 16],
                                vb[0:32, :].rearrange("p (ee t) -> p t ee", t=64)[:, ugl, :],
                                a1[:, r * 512 + sl_tok * 32: r * 512 + sl_tok * 32 + 16],
                                start=True, stop=False), pe, 0)
                            qi("tensor", lambda e, ob=obank, vb=vbuf, ugl=ugl, a2=a2, r=r, sl_tok=sl_tok, pairtok=pairtok: e.matmul(
                                ob[:, pairtok * 16:(pairtok + 1) * 16],
                                vb[0:32, :].rearrange("p (ee t) -> p t ee", t=64)[:, ugl, :],
                                a2[:, r * 512 + sl_tok * 32 + 16: r * 512 + sl_tok * 32 + 32],
                                start=False, stop=True),
                                pe, 1 if sl_tok == 15 else 0)
                    pe_diffv[p] = pe.n
                    qw("vector", pe, pe_diffv[p])
                    qi("vector", lambda e, ob=obank: e.tensor_copy(o_sb[:, :], ob[:, :]), dv, 1)
                    dv_ocp = dv.n
                    if _DEBUG and ms == 0 and sl == 0 and p == 0:
                        dbg("at1_p0", at1[:, :], waits=[(dv, dv_at[(0, 1)])])
                        dbg("at2_p0", at2[:, :], waits=[(dv, dv_at[(0, 1)])])
                        dbg("osb_p0", o_sb[:, :], waits=[(dv, dv_ocp)])
                    bank_done(obank, dv)
                    qw("scalar", dv, dv_ocp)
                    if p >= 1:
                        qw("scalar", pe, st_o[p - 1]["pe_msq"])
                    qi("scalar", lambda e: e.activation(o_sq[:, :], o_sb[:, :], AF.Square), ac, 1)
                    ac_osq = ac.n
                    mb = obk[p % 2]
                    bank_gate(mb)
                    qw("tensor", ac, ac_osq)
                    qi("tensor", lambda e, mb=mb: e.matmul(mb[0:1, :], onescb[:, :], o_sq[:, :], start=True, stop=True), pe, 1)
                    pe_msq = pe.n
                    qw("scalar", pe, pe_msq)
                    qi("scalar", lambda e, mb=mb: e.activation(msq_t[0:1, :], mb[0:1, :], AF.Ln, bias=cEPS[0:1, :], scale=1.0 / EH), ac, 1)
                    qi("scalar", lambda e: e.activation(rstd1[0:1, :], msq_t[0:1, :], AF.Exp, scale=-0.5, bias=cLN1M[0:1, :]), ac, 1)
                    ac_rst = ac.n
                    qw("tensor", ac, ac_rst)
                    qi("tensor", lambda e, mb=mb: e.matmul(mb[:, :], ones1f[:, :], rstd1[0:1, :], start=True, stop=True), pe, 1)
                    st_o[p] = {"pe_msq": pe_msq, "pe_bc": pe.n}
            dv_att_done = dv.n
            if sl == 0:
                dbg("ost_ms%d" % ms, ost[:, :], waits=[(dv, dv_att_done)])

            qw("sync", pe)
            for dq in range(4):
                src0 = d_wot[:, dq * 512:(dq + 1) * 512].rearrange("(a p) d -> p a d", p=128)
                wdst0, ww0, wi0 = wpiece(src0[:, 0:8, :], 8, 512, cache=(3, dq * 2))
                wdst1, ww1, wi1 = wpiece(src0[:, 8:16, :], 8, 512, cache=(3, dq * 2 + 1))
                qw("tensor", ww0[0], ww0[1])
                qw("tensor", ww1[0], ww1[1])
                for tsub in range(2):
                    t0g = st0 + tsub * 128
                    slot = tsub
                    qw("sync", dv, slot_guard[slot][0])
                    qi("sync", lambda e, t0g=t0g, dq=dq, slot=slot: e.dma_start(
                        out=xslice[:, slot * 512:(slot + 1) * 512],
                        in_=d_x[t0g:t0g + 128, dq * 512:(dq + 1) * 512]), lds[slot], 16)
                    ld_xs = lds[slot].n
                    bank = [pB, pC][tsub]
                    bank_gate(bank)
                    qw("tensor", dv, dv_att_done)
                    for a in range(16):
                        w_ = wdst0 if a < 8 else wdst1
                        qi("tensor", lambda e, b=bank, a=a, tsub=tsub, w=w_, ostv=ost: e.matmul(
                            b[:, :],
                            ostv[:, a * SLAB + tsub * 128: a * SLAB + tsub * 128 + 128],
                            w.rearrange("p (a d) -> p a d", a=8)[:, a % 8, :],
                            start=(a == 0), stop=False), pe, 0)
                    qi("tensor", lambda e, b=bank, dq=dq: e.matmul(
                        b[:, :], ones1b[:, :], bo_sb[0:1, dq * 512:(dq + 1) * 512],
                        start=False, stop=True), pe, 1)
                    pe_wo = pe.n
                    qw("vector", pe, pe_wo)
                    qw("vector", lds[slot], ld_xs)
                    qw("vector", sts[slot], slot_guard[slot][1])
                    gi = sl * 2 + tsub
                    qw("vector", ac, ac_silu[gi])
                    qi("vector", lambda e, b=bank, gi=gi, dq=dq, slot=slot: e.tensor_mul(
                        outsl[:, slot * 512:(slot + 1) * 512],
                        b[:, :], gbuf[:, gi * 2048 + dq * 512: gi * 2048 + (dq + 1) * 512]), dv, 1)
                    dv_out = qi("vector", lambda e, slot=slot: e.tensor_add(
                        outsl[:, slot * 512:(slot + 1) * 512],
                        outsl[:, slot * 512:(slot + 1) * 512],
                        xslice[:, slot * 512:(slot + 1) * 512]), dv, 1)
                    bank_done(bank, dv)
                    qw("sync", dv, dv_out)
                    st_tgt = qi("sync", lambda e, t0g=t0g, dq=dq, slot=slot: e.dma_start(
                        out=d_out[t0g:t0g + 128, dq * 512:(dq + 1) * 512],
                        in_=outsl[:, slot * 512:(slot + 1) * 512]), sts[slot], 16)
                    slot_guard[slot] = (dv_out, st_tgt)
                wdone(wi0)
                wdone(wi1)

    qw("sync", st)
    qw("sync", sts[0])
    qw("sync", sts[1])
    qw("sync", vda[0])
    qw("sync", vda[1])

    semctx = ExitStack()
    for s in allsems:
        s.h = semctx.enter_context(nc.semaphore("sem_" + s.name))
    with nc.Block() as blk:
        blk.sync(lambda e: [f(e) for f in Q["sync"]])
        blk.gpsimd(lambda e: [f(e) for f in Q["gpsimd"]])
        blk.vector(lambda e: [f(e) for f in Q["vector"]])
        blk.scalar(lambda e: [f(e) for f in Q["scalar"]])
        blk.tensor(lambda e: [f(e) for f in Q["tensor"]])
    semctx.close()
    ctx.close()
    return nc, dbg_outs


def _host_inputs(inputs):
    f32 = np.float32
    g = lambda k: np.asarray(inputs[k], dtype=f32)
    perm = np.zeros(D, dtype=np.int64)
    for a in range(32):
        c, h = divmod(a, 16)
        for d in range(DH):
            perm[a * DH + d] = h * 128 + 2 * d + c
    shared = {
        "wqt": np.ascontiguousarray(g("L0_attn_Wq")[perm, :].T),
        "wkt": np.ascontiguousarray(g("L0_attn_Wk")[perm, :].T),
        "wvt": np.ascontiguousarray(g("L0_attn_Wv").T),
        "wot": np.ascontiguousarray(g("L0_attn_Wo").T),
        "bo": np.ascontiguousarray(g("L0_attn_bo").reshape(1, D)),
        "lq1": np.ascontiguousarray(g("L0_attn_lq1").reshape(1, DH)),
        "lq2": np.ascontiguousarray(g("L0_attn_lq2").reshape(1, DH)),
        "lk1": np.ascontiguousarray(g("L0_attn_lk1").reshape(1, DH)),
        "lk2": np.ascontiguousarray(g("L0_attn_lk2").reshape(1, DH)),
        "wa": np.ascontiguousarray(g("L0_ffn_wa")),
        "rg": np.ascontiguousarray(g("L0_ffn_rg")),
        "rgb": np.ascontiguousarray(g("L0_ffn_rgb")),
        "rw1": np.ascontiguousarray(g("L0_ffn_rw1")),
        "rb1": np.ascontiguousarray(g("L0_ffn_rb1")),
        "rw2": np.ascontiguousarray(g("L0_ffn_rw2")),
        "rb2": np.ascontiguousarray(g("L0_ffn_rb2")),
        "sg": np.ascontiguousarray(g("L0_ffn_sg")),
        "sgb": np.ascontiguousarray(g("L0_ffn_sgb")),
        "sw1": np.ascontiguousarray(g("L0_ffn_sw1")),
        "sb1": np.ascontiguousarray(g("L0_ffn_sb1")),
        "sw2": np.ascontiguousarray(g("L0_ffn_sw2")),
        "sb2": np.ascontiguousarray(g("L0_ffn_sb2")),
        "i128b": np.eye(128).astype(ml_dtypes.bfloat16),
        "i128f": np.eye(128, dtype=f32),
        "i16f": np.eye(16, dtype=f32),
        "ind_m": (-((np.arange(128) % 32) >= 16).astype(f32)).reshape(128, 1),
        "ind_1m": (1.0 - ((np.arange(128) % 32) >= 16).astype(f32)).reshape(128, 1),
    }
    return shared


_LAST_RESULTS = None


def kernel(**inputs):
    global _LAST_RESULTS
    x = np.asarray(inputs["L0_attn_x"], dtype=np.float32).reshape(16384, D)
    shared = _host_inputs(inputs)
    nc, dbg_outs = build_program()
    in_maps = []
    for c in range(NCORES):
        m = dict(shared)
        m["x"] = np.ascontiguousarray(x[c * T:(c + 1) * T, :])
        in_maps.append(m)
    res = run_bass_kernel_spmd(nc, in_maps, core_ids=list(range(NCORES)))
    _LAST_RESULTS = res.results
    out = np.concatenate([res.results[c]["out"] for c in range(NCORES)], axis=0)
    return out.reshape(4, 4096, D)
